# version 01
# baseline (speedup 1.0000x reference)
import numpy as np
from contextlib import ExitStack

import concourse.bass as bass
import concourse.mybir as mybir
from concourse.bass_utils import run_bass_kernel_spmd

F32 = mybir.dt.float32
BF16 = mybir.dt.bfloat16
AF = mybir.ActivationFunctionType
ALU = mybir.AluOpType
AX = mybir.AxisListType

D = 1024
T = 4096
SEQ = 2048
J = 4096
TBW = 512
NTB = 8
KT = 8
FT = 8
JT = 32
HP = 8
QQ = 4
TC = 16
C_ROUND = float(np.float32(3 << 22))
QP = 127.0
MIN_RANGE = 1e-6
LN_EPS = 1e-5

KB = 1024
S0, S1, S2, S3 = 0, 16 * KB, 32 * KB, 48 * KB
XQR = 64 * KB
WQR = 128 * KB
ARENA = 192 * KB


class Prog:
    def __init__(self, nc, ctx, paranoid=False):
        self.nc = nc
        self.ctx = ctx
        self.ops = []
        self.sems = {}
        self.counts = {}
        self.paranoid = paranoid

    def sem(self, name):
        if name not in self.sems:
            self.sems[name] = self.ctx.enter_context(self.nc.semaphore(name))
            self.counts[name] = 0
        return self.sems[name]

    def n(self, name):
        return self.counts.get(name, 0)

    def do(self, engine, emit, waits=(), incs=()):
        if self.paranoid:
            waits = list(waits) + [(s, v) for s, v in self.counts.items()]
        waits = [(s, v) for (s, v) in waits if v > 0]
        wd = {}
        for s, v in waits:
            wd[s] = max(wd.get(s, 0), v)
        waits = list(wd.items())
        for s, _ in waits:
            self.sem(s)
        assert len(incs) <= 1
        for s, k in incs:
            self.sem(s)
            self.counts[s] += (k if k is not None else 1)
        self.ops.append((engine, emit, waits, list(incs)))

    def replay(self, block):
        def mk(eng_name):
            def body(eng):
                for engine, emit, waits, incs in self.ops:
                    if engine != eng_name:
                        continue
                    for s, v in waits:
                        eng.wait_ge(self.sems[s], v)
                    ins = emit(eng)
                    for s, k in incs:
                        assert ins is not None
                        if k is None:
                            ins.then_inc(self.sems[s])
                        else:
                            ins.then_inc(self.sems[s], k)
            return body
        block.sync(mk("sync"))
        block.tensor(mk("tensor"))
        block.scalar(mk("scalar"))
        block.vector(mk("vector"))
        block.gpsimd(mk("gpsimd"))


def build_nc(phases=("stats", "proj", "xq", "attn", "ffn1", "ffn2"), dbg_out=(), paranoid=()):
    nc = bass.Bass("TRN2", target_bir_lowering=False)

    xT = nc.dram_tensor("xT", [D, T], F32, kind="ExternalInput")
    Wq = nc.dram_tensor("Wq", [D, D], F32, kind="ExternalInput")
    Wkv = nc.dram_tensor("Wkv", [D, D], F32, kind="ExternalInput")
    Wout = nc.dram_tensor("Wout", [D, D], F32, kind="ExternalInput")
    w1T = nc.dram_tensor("w1T", [D, J], F32, kind="ExternalInput")
    w2T = nc.dram_tensor("w2T", [J, D], F32, kind="ExternalInput")
    b1m = nc.dram_tensor("b1m", [128, 32], F32, kind="ExternalInput")
    b2m = nc.dram_tensor("b2m", [128, 8], F32, kind="ExternalInput")
    boutm = nc.dram_tensor("boutm", [128, 8], F32, kind="ExternalInput")
    identF = nc.dram_tensor("identF", [128, 128], F32, kind="ExternalInput")
    identB = nc.dram_tensor("identB", [128, 128], BF16, kind="ExternalInput")
    yT = nc.dram_tensor("yT", [D, T], F32, kind="ExternalOutput")

    def scratch(name, shape, dtype):
        kind = "ExternalOutput" if name in dbg_out else "Internal"
        return nc.dram_tensor(name, shape, dtype, kind=kind)

    qs = scratch("qs", [FT, 128, T], BF16)
    kvs = scratch("kvs", [FT, 128, T], BF16)
    kts = scratch("kts", [2, HP, TC, 128, 128], BF16)
    wouts = scratch("wouts", [8, 8, 128, 128], BF16)
    hsp = scratch("hsp", [JT, 128, T], F32)
    att_s = scratch("att_s", [2, 8, 128, SEQ], BF16)
    mxs = scratch("mxs", [1, 16], F32)
    dxq = scratch("dxq", [KT, 128, T], BF16)
    dmaxa = scratch("dmaxa", [128, 1], F32)
    dw1q = scratch("dw1q", [KT, 128, J], BF16)
    ccin1 = nc.dram_tensor("ccin1", [128, 16], F32)
    ccout1 = nc.dram_tensor("ccout1", [128, 16], F32)
    dbgcc = scratch("dbgcc", [128, 32], F32)
    ccin2 = nc.dram_tensor("ccin2", [128, 16], F32)
    ccout2 = nc.dram_tensor("ccout2", [128, 16], F32)

    with ExitStack() as ctx:
        arena = ctx.enter_context(nc.sbuf_tensor("arena", [128, ARENA], mybir.dt.uint8))
        base = nc.lookup_mloc(arena).addr

        def at(name, shape, dtype, off):
            return nc.alloc_sbuf_tensor_at(name, shape, dtype, offset=base + off)

        def sbt(name, shape, dtype):
            return ctx.enter_context(nc.sbuf_tensor(name, shape, dtype))

        ones1 = sbt("ones1", [1, 128], F32)
        onesC = sbt("onesC", [128, 1], F32)
        idF = sbt("idF", [128, 128], F32)
        idB = sbt("idB", [128, 128], BF16)
        b1T = sbt("b1T", [128, 32], F32)
        b2T = sbt("b2T", [128, 8], F32)
        boutT = sbt("boutT", [128, 8], F32)
        b1s = sbt("b1s", [128, 32], F32)
        b2s = sbt("b2s", [128, 8], F32)
        maxacc = sbt("maxacc", [128, 1], F32)
        w1acc = sbt("w1acc", [128, 1], F32)
        w2acc = sbt("w2acc", [128, 1], F32)
        hacc = sbt("hacc", [128, 1], F32)
        tmpc = sbt("tmpc", [128, 1], F32)
        epsB = sbt("epsB", [128, 1], F32)
        mrow = sbt("mrow", [1, 16], F32)
        s1B = sbt("s1B", [128, 1], F32)
        s2B = sbt("s2B", [128, 1], F32)
        rw1B = sbt("rw1B", [128, 1], F32)
        rw2B = sbt("rw2B", [128, 1], F32)
        rhB = sbt("rhB", [128, 1], F32)
        ccs = sbt("ccs", [128, 16], F32)
        rowb = sbt("rowb", [1, 128], F32)
        mtmp = sbt("mtmp", [128, 1], F32)
        mcl = sbt("mcl", [128, 1], F32)
        mrc = sbt("mrc", [128, 1], F32)
        rB = sbt("rB", [128, 1], F32)
        sxB = sbt("sxB", [128, 1], F32)
        shB = sbt("shB", [128, 1], F32)
        sw1B = sbt("sw1B", [128, 1], F32)
        sw2B = sbt("sw2B", [128, 1], F32)
        ccr = sbt("ccr", [128, 16], F32)
        stbuf = [sbt(f"stbuf{i}", [128, 1024], BF16) for i in range(2)]
        aos = [sbt(f"aos{i}", [128, TBW], BF16) for i in range(2)]
        ktt = [sbt(f"ktt{i}", [128, TBW], BF16) for i in range(2)]

        PSA = ctx.enter_context(nc.psum_tensor("PSA", [128, 2560], F32))
        PSB = ctx.enter_context(nc.psum_tensor("PSB", [128, 512], F32))
        PST = ctx.enter_context(nc.psum_tensor("PST", [128, 2048], BF16))

        def bank(i):
            return PSA[:, i * 512:(i + 1) * 512]

        def pmax_chain(pref, src, mslot, rB_t, sB_t, init_waits):
            P.do("tensor", lambda t: t.transpose(PSB[0:1, 128:256], src, idF[:]),
                 waits=init_waits)
            P.do("tensor", lambda t: t.matmul(
                 PSB[0:1, 0:1], ones1[0:1, 0:1], ones1[0:1, 0:1], start=True, stop=True),
                 incs=[(pref + "fx", 1)])
            P.do("vector", lambda v: v.tensor_copy(out=rowb[:], in_=PSB[0:1, 128:256]),
                 waits=[(pref + "fx", 1)], incs=[(pref + "rd", 1)])
            P.do("vector", lambda v, mslot=mslot: v.reduce_max(
                 out=mrow[0:1, mslot:mslot + 1], in_=rowb[:], axis=AX.X),
                 waits=[(pref + "rd", 1)], incs=[(pref + "rd", 1)])
            P.do("tensor", lambda t, mslot=mslot: t.matmul(
                 PSB[:, 256:257], ones1[:], mrow[0:1, mslot:mslot + 1], start=True, stop=True),
                 waits=[(pref + "rd", 2)])
            P.do("tensor", lambda t: t.matmul(
                 PSB[0:1, 384:385], ones1[0:1, 0:1], ones1[0:1, 0:1], start=True, stop=True),
                 incs=[(pref + "f2", 1)])
            P.do("vector", lambda v: v.tensor_copy(out=mtmp[:], in_=PSB[:, 256:257]),
                 waits=[(pref + "f2", 1)], incs=[(pref + "ch", 1)])
            P.do("vector", lambda v: v.tensor_scalar_max(out=mcl[:], in0=mtmp[:], scalar1=MIN_RANGE),
                 waits=[(pref + "ch", 1)], incs=[(pref + "ch", 1)])
            P.do("vector", lambda v: v.reciprocal(out=mrc[:], in_=mcl[:]),
                 waits=[(pref + "ch", 2)], incs=[(pref + "ch", 1)])
            P.do("vector", lambda v, rB_t=rB_t: v.tensor_scalar_mul(out=rB_t[:], in0=mrc[:], scalar1=QP),
                 waits=[(pref + "ch", 3)], incs=[(pref + "ch", 1)])
            P.do("vector", lambda v, sB_t=sB_t: v.tensor_scalar_mul(out=sB_t[:], in0=mcl[:], scalar1=1.0 / QP),
                 waits=[(pref + "ch", 4)], incs=[(pref + "ch", 1)])

        xs = [at("xs0", [128, T], F32, S0), at("xs1", [128, T], F32, S1)]
        sumrow = at("sumrow", [1, T], F32, WQR)
        sqrow = at("sqrow", [1, T], F32, WQR + 16 * KB)
        vrow = at("vrow", [1, T], F32, WQR + 32 * KB)
        sqb = [at("sqb0", [128, TBW], F32, XQR + 32 * KB),
               at("sqb1", [128, TBW], F32, XQR + 34 * KB)]
        xnb = [at("xnb0", [128, KT, TBW], BF16, S2),
               at("xnb1", [128, KT, TBW], BF16, S2 + 8 * KB)]
        c0B = at("c0B", [128, TBW], F32, S3)
        c1B = at("c1B", [128, TBW], F32, S3 + 2 * KB)
        tmpf = [at("tmpf0", [128, TBW], F32, S3 + 8 * KB),
                at("tmpf1", [128, TBW], F32, S3 + 10 * KB)]
        qst = [at("qst0", [128, TBW], BF16, S3 + 12 * KB),
               at("qst1", [128, TBW], BF16, S3 + 13 * KB)]
        kvt = [at("kvt0", [128, TBW], BF16, S3 + 14 * KB),
               at("kvt1", [128, TBW], BF16, S3 + 15 * KB)]
        xqv = [at(f"xq{k}", [128, T], BF16, XQR + k * 8 * KB) for k in range(KT)]
        Wqb = [at(f"Wqb{k}", [128, D], BF16, XQR + k * 2 * KB) for k in range(KT)]
        Wkvb = [at(f"Wkvb{k}", [128, D], BF16, XQR + 16 * KB + k * 2 * KB) for k in range(KT)]
        qhp = [at("qhp0", [128, SEQ], BF16, S0), at("qhp1", [128, SEQ], BF16, S0 + 4 * KB)]
        khp = [at("khp0", [128, SEQ], BF16, S0 + 8 * KB), at("khp1", [128, SEQ], BF16, S0 + 12 * KB)]
        tokhp = [at("tok0", [128, TC, 128], BF16, S1), at("tok1", [128, TC, 128], BF16, S1 + 4 * KB)]
        outTT = [at(f"oTT{f}", [128, SEQ], BF16, S2 + f * 4 * KB) for f in range(FT)]
        wst = [at("wst0", [128, 8, 128], BF16, S1 + 8 * KB), at("wst1", [128, 8, 128], BF16, S1 + 10 * KB)]
        wls = [at("wls0", [128, J], F32, S0), at("wls1", [128, J], F32, S1)]
        w1q = [at(f"w1q{k}", [128, J], BF16, WQR + k * 8 * KB) for k in range(KT)]
        hst = [at(f"hst{i}", [128, TBW], F32, S2 + i * 2 * KB) for i in range(4)]
        hld = [at(f"hld{i}", [128, 2048], F32, S0 + i * 8 * KB) for i in range(2)]
        hqv = at("hq", [128, 16384], BF16, S1)
        w2q = [at(f"w2q{j}", [128, D], BF16, WQR + j * 2 * KB) for j in range(JT)]
        w2ls = [at("w2ls0", [128, D], F32, S3), at("w2ls1", [128, D], F32, S3 + 4 * KB)]
        atts = at("atts", [128, 8, TBW], BF16, S3)
        xsl = [at("xsl0", [128, TBW], F32, S3 + 8 * KB), at("xsl1", [128, TBW], F32, S3 + 10 * KB)]
        yst = [at("yst0", [128, TBW], F32, S3 + 12 * KB), at("yst1", [128, TBW], F32, S3 + 14 * KB)]

        P = Prog(nc, ctx, paranoid=False)
        paranoid = set(phases if paranoid is True else (paranoid or ()))
        xTr = xT.rearrange("(k p) (n w) -> n p k w", p=128, w=TBW)

        def x_load(buf, tb, waits):
            P.do("sync", lambda s, buf=buf, tb=tb: s.dma_start(
                out=xs[buf].rearrange("p (k w) -> p k w", w=TBW), in_=xTr[tb]),
                waits=waits, incs=[("xl", 16)])

        for t_ in (maxacc, w1acc, w2acc, hacc):
            P.do("vector", lambda v, t_=t_: v.memset(t_[:], 0.0))
        P.do("vector", lambda v: v.memset(epsB[:], LN_EPS), incs=[("cst", 1)])
        P.do("vector", lambda v: v.memset(ones1[:], 1.0))
        P.do("vector", lambda v: v.memset(onesC[:], 1.0))
        P.do("vector", lambda v: v.memset(ccs[:], 0.0), incs=[("cst", 1)])
        P.do("sync", lambda s: s.dma_start(out=idF[:], in_=identF[:]), incs=[("ini", 16)])
        P.do("sync", lambda s: s.dma_start(out=idB[:], in_=identB[:]), incs=[("ini", 16)])
        P.do("sync", lambda s: s.dma_start(out=b1T[:], in_=b1m[:]), incs=[("bte", 16)])
        P.do("sync", lambda s: s.dma_start(out=b2T[:], in_=b2m[:]), incs=[("bte", 16)])
        P.do("sync", lambda s: s.dma_start(out=boutT[:], in_=boutm[:]), incs=[("bte", 16)])
        for k in range(KT):
            P.do("gpsimd", lambda g, k=k: g.dma_start(
                out=Wqb[k][:], in_=Wq[k * 128:(k + 1) * 128, :]), incs=[("wld", 16)])
            P.do("gpsimd", lambda g, k=k: g.dma_start(
                out=Wkvb[k][:], in_=Wkv[k * 128:(k + 1) * 128, :]), incs=[("wld", 16)])
        for k in range(KT):
            P.do("gpsimd", lambda g, k=k: g.dma_start(
                out=wouts[:, k], in_=Wout[k * 128:(k + 1) * 128, :]
                .rearrange("p (dt d) -> dt p d", d=128)), incs=[("wos", 16)])

        P.paranoid = "stats" in paranoid
        if "stats" in phases:
            xl0 = P.n("xl")
            for tb in range(NTB):
                pb = tb % 2
                x_load(pb, tb, [("smm", max(0, 16 * (tb - 1))), ("sqd", max(0, 8 * (tb - 1)))])
                for k in range(KT):
                    sl = k % 2
                    P.do("scalar", lambda a, pb=pb, k=k, sl=sl: a.square(
                        out=sqb[sl][:], in_=xs[pb][:, k * TBW:(k + 1) * TBW]),
                        waits=[("xl", xl0 + 16 * (tb + 1)), ("smm", max(0, 16 * tb + 2 * k - 2))],
                        incs=[("sqd", 1)])
                    P.do("tensor", lambda t, tb=tb, pb=pb, k=k: t.matmul(
                        PSA[0:1, (tb % 2) * 512:(tb % 2) * 512 + 512],
                        onesC[:], xs[pb][:, k * TBW:(k + 1) * TBW],
                        start=(k == 0), stop=(k == KT - 1)),
                        waits=[("xl", xl0 + 16 * (tb + 1)), ("sev", max(0, 2 * (tb - 1)))],
                        incs=[("smm", 1)])
                    P.do("tensor", lambda t, tb=tb, k=k, sl=sl: t.matmul(
                        PSA[0:1, 1024 + (tb % 2) * 512:1024 + (tb % 2) * 512 + 512],
                        onesC[:], sqb[sl][:],
                        start=(k == 0), stop=(k == KT - 1)),
                        waits=[("sqd", 8 * tb + k + 1), ("sev", max(0, 2 * (tb - 1)))],
                        incs=[("smm", 1)])
                P.do("vector", lambda v, tb=tb: v.tensor_copy(
                    out=sumrow[0:1, tb * TBW:(tb + 1) * TBW],
                    in_=PSA[0:1, (tb % 2) * 512:(tb % 2) * 512 + 512]),
                    waits=[("smm", 16 * (tb + 1))], incs=[("sev", 1)])
                P.do("vector", lambda v, tb=tb: v.tensor_copy(
                    out=sqrow[0:1, tb * TBW:(tb + 1) * TBW],
                    in_=PSA[0:1, 1024 + (tb % 2) * 512:1024 + (tb % 2) * 512 + 512]),
                    waits=[("smm", 16 * (tb + 1))], incs=[("sev", 1)])
            P.do("vector", lambda v: v.tensor_scalar_mul(out=sumrow[:], in0=sumrow[:], scalar1=1.0 / D),
                 waits=[("sev", 16)], incs=[("rch", 1)])
            P.do("vector", lambda v: v.tensor_scalar_mul(out=sqrow[:], in0=sqrow[:], scalar1=1.0 / D),
                 waits=[("rch", 1)], incs=[("rch", 1)])
            P.do("vector", lambda v: v.tensor_mul(out=vrow[:], in0=sumrow[:], in1=sumrow[:]),
                 waits=[("xl", xl0 + 16 * NTB), ("rch", 2)], incs=[("rch", 1)])
            P.do("vector", lambda v: v.tensor_sub(out=sqrow[:], in0=sqrow[:], in1=vrow[:]),
                 waits=[("rch", 3)], incs=[("rch", 1)])
            P.do("scalar", lambda a: a.activation(out=vrow[:], in_=sqrow[:], func=AF.Sqrt, bias=epsB[0:1, 0:1]),
                 waits=[("rch", 4), ("cst", 1)], incs=[("rch", 1)])
            P.do("vector", lambda v: v.reciprocal(out=sqrow[:], in_=vrow[:]),
                 waits=[("rch", 5)], incs=[("rch", 1)])
            P.do("vector", lambda v: v.scalar_tensor_tensor(
                 out=sumrow[:], in0=sumrow[:], scalar=-1.0, in1=sqrow[:], op0=ALU.mult, op1=ALU.mult),
                 waits=[("rch", 6)], incs=[("rch", 1)])


        P.paranoid = "proj" in paranoid
        if "proj" in phases:
            xl0 = P.n("xl")
            for tb in range(NTB):
                pb = tb % 2
                b = tb // 4
                x_load(pb, tb, [("pt1", max(0, 8 * (tb - 1))), ("rch", 7 if "stats" in phases else 0)])
                P.do("tensor", lambda t, tb=tb: t.matmul(
                     bank(3), ones1[:], sumrow[0:1, tb * TBW:(tb + 1) * TBW], start=True, stop=True),
                     waits=[("rch", 7 if "stats" in phases else 0), ("pbe", max(0, 2 * tb - 1))],
                     incs=[("pbc", 1)])
                P.do("tensor", lambda t, tb=tb: t.matmul(
                     bank(4), ones1[:], sqrow[0:1, tb * TBW:(tb + 1) * TBW], start=True, stop=True),
                     waits=[("pbe", max(0, 2 * tb))],
                     incs=[("pbc", 1)])
                P.do("vector", lambda v: v.tensor_copy(out=c0B[:], in_=bank(3)),
                     waits=[("pbc", 2 * tb + 1)], incs=[("pbe", 1)])
                P.do("vector", lambda v: v.tensor_copy(out=c1B[:], in_=bank(4)),
                     waits=[("pbc", 2 * tb + 2)], incs=[("pbe", 1)])
                for k in range(KT):
                    tsl = k % 2
                    m = 8 * tb + k
                    P.do("vector", lambda v, pb=pb, k=k, tsl=tsl: v.tensor_mul(
                        out=tmpf[tsl][:], in0=xs[pb][:, k * TBW:(k + 1) * TBW], in1=c1B[:]),
                        waits=[("xl", xl0 + 16 * (tb + 1)), ("pbe", 2 * tb + 2),
                               ("pmx", max(0, m - 1))],
                        incs=[("pt1", 1)])
                    P.do("vector", lambda v, tsl=tsl: v.tensor_add(
                        out=tmpf[tsl][:], in0=tmpf[tsl][:], in1=c0B[:]),
                        waits=[("pt1", 8 * tb + k + 1)], incs=[("pt2", 1)])
                    P.do("vector", lambda v, pb=pb, k=k, tsl=tsl: v.tensor_copy(
                        out=xnb[pb][:, k, :], in_=tmpf[tsl][:]),
                        waits=[("pmm", max(0, 128 * (tb - 1))), ("pt2", 8 * tb + k + 1)],
                        incs=[("pxn", 1)])
                    P.do("vector", lambda v, tsl=tsl: v.tensor_reduce(
                        out=tmpc[:], in_=tmpf[tsl][:], axis=AX.X, op=ALU.max,
                        apply_absolute_value=True),
                        waits=[("pt2", 8 * tb + k + 1)], incs=[("pmr", 1)])
                    P.do("vector", lambda v: v.tensor_max(out=maxacc[:], in0=maxacc[:], in1=tmpc[:]),
                         waits=[("pmr", 8 * tb + k + 1)], incs=[("pmx", 1)])
                for w in range(2):
                    for f in range(FT):
                        g = tb * 16 + w * 8 + f
                        m = tb * 8 + f
                        bk = g % 3
                        for k in range(KT):
                            wv = Wqb if w == 0 else Wkvb
                            P.do("tensor", lambda t, pb=pb, f=f, k=k, bk=bk, wv=wv: t.matmul(
                                bank(bk), wv[k][:, f * 128:(f + 1) * 128], xnb[pb][:, k, :],
                                start=(k == 0), stop=(k == KT - 1)),
                                waits=[("pxn", 8 * tb + k + 1), ("wld", 256),
                                       ("pev", max(0, g - 2))],
                                incs=[("pmm", 1)])
                        ssl = m % 2
                        stg = qst if w == 0 else kvt
                        if w == 0:
                            sp_prev = ((m - 2) // 8) * 16 + ((m - 2) % 8) + 1 if m >= 2 else 0
                            extra = []
                        else:
                            sp_prev = ((m - 2) // 8) * 16 + 8 + ((m - 2) % 8) + 1 if m >= 2 else 0
                            extra = [("ptr", max(0, m - 1))]
                        P.do("scalar", lambda a, bk=bk, ssl=ssl, stg=stg, w=w: a.activation(
                            out=stg[ssl][:], in_=bank(bk), func=AF.Copy,
                            scale=(0.125 if w == 0 else 1.0)),
                            waits=[("pmm", 8 * (g + 1)), ("psp", 16 * sp_prev)] + extra,
                            incs=[("pev", 1)])
                        dst = qs if w == 0 else kvs
                        P.do("sync", lambda s, f=f, tb=tb, ssl=ssl, stg=stg, dst=dst: s.dma_start(
                            out=dst[f][:, tb * TBW:(tb + 1) * TBW], in_=stg[ssl][:]),
                            waits=[("pev", g + 1)], incs=[("psp", 16)])
                        if w == 1:
                            ph = m % 2
                            for i in range(4):
                                P.do("tensor", lambda t, ssl=ssl, i=i, ph=ph: t.transpose(
                                    PST[:, ph * 1024 + i * 128: ph * 1024 + (i + 1) * 128],
                                    kvt[ssl][:, i * 128:(i + 1) * 128], idB[:]),
                                    waits=[("pev", g + 1), ("pte", max(0, m - 1))])
                            P.do("tensor", lambda t: t.matmul(
                                PSB[0:1, 0:1], ones1[0:1, 0:1], ones1[0:1, 0:1],
                                start=True, stop=True),
                                incs=[("ptr", 1)])
                            P.do("vector", lambda v, ph=ph, ksl=m % 2: v.tensor_copy(
                                out=ktt[ksl][:], in_=PST[:, ph * 1024: ph * 1024 + 512]),
                                waits=[("ptr", m + 1), ("pts", max(0, 16 * (m - 1)))],
                                incs=[("pte", 1)])
                            P.do("sync", lambda s, b=b, f=f, tb=tb, ksl=m % 2: s.dma_start(
                                out=kts[b, f, (tb % 4) * 4:(tb % 4) * 4 + 4].rearrange("c p d -> p c d"),
                                in_=ktt[ksl].rearrange("p (c d) -> p c d", d=128)),
                                waits=[("pte", m + 1)], incs=[("pts", 16)])
            if "dmaxa" in dbg_out:
                P.do("sync", lambda s: s.dma_start(out=dmaxa[:], in_=maxacc[:]),
                     waits=[("pmx", 64)], incs=[("dmz", 16)])
            P.do("vector", lambda v: v.tensor_copy(out=ccs[:, 0:1], in_=maxacc[:]),
                 waits=[("pmx", 64), ("cst", 2)], incs=[("gmx", 1)])
            P.do("sync", lambda s: s.dma_start(out=ccin1[:], in_=ccs[:]),
                 waits=[("gmx", 1)], incs=[("gst", 16)])
            P.do("gpsimd", lambda g: g.collective_compute(
                "AllReduce", ALU.max, replica_groups=[list(range(8))],
                ins=[ccin1[:]], outs=[ccout1[:]]),
                waits=[("gst", 16)], incs=[("cc1", None)])
            P.do("sync", lambda s: s.dma_start(out=ccr[:], in_=ccout1[:]),
                 waits=[("cc1", 1)], incs=[("gld", 16)])
            if "dbgcc" in dbg_out:
                P.do("sync", lambda s: s.dma_start(out=dbgcc[:, 0:16], in_=ccs[:]),
                     waits=[("gmx", 1)], incs=[("dcc", 16)])
                P.do("sync", lambda s: s.dma_start(out=dbgcc[:, 16:32], in_=ccr[:]),
                     waits=[("gld", 16)], incs=[("dcc", 16)])
            pmax_chain("a1", ccr[:, 0:1], 0, rB, sxB,
                       [("gld", 16), ("pev", 128)])
            P.do("vector", lambda v: v.tensor_copy(out=mrow[0:1, 12:13], in_=rB[0:1, :]),
                 waits=[("a1ch", 5)], incs=[("a1d", 1)])
            P.do("vector", lambda v: v.tensor_copy(out=mrow[0:1, 13:14], in_=sxB[0:1, :]),
                 incs=[("a1d", 1)])
            P.do("vector", lambda v: v.tensor_copy(out=mrow[0:1, 14:15], in_=mtmp[0:1, :]),
                 incs=[("a1d", 1)])
            P.do("vector", lambda v: v.tensor_copy(out=mrow[0:1, 15:16], in_=rB[64:65, :]),
                 incs=[("a1d", 1)])

        P.paranoid = "xq" in paranoid
        if "xq" in phases:
            nproj_mm = P.n("pmm")
            xl0 = P.n("xl")
            P.do("vector", lambda v: v.tensor_scalar_mul(out=sumrow[:], in0=sumrow[:],
                                                         scalar1=rB[0:1, 0:1]),
                 waits=[("pbc", 16 if "proj" in phases else 0), ("a1ch", 5)], incs=[("qcs", 1)])
            P.do("vector", lambda v: v.tensor_scalar_mul(out=sqrow[:], in0=sqrow[:],
                                                         scalar1=rB[0:1, 0:1]),
                 waits=[("qcs", 1)], incs=[("qcs", 1)])
            for tb in range(NTB):
                pb = tb % 2
                x_load(pb, tb, [("xqd", max(0, 8 * (tb - 1))), ("pt1", 64 if "proj" in phases else 0)])
                P.do("tensor", lambda t, tb=tb: t.matmul(
                     bank(3), ones1[:], sumrow[0:1, tb * TBW:(tb + 1) * TBW], start=True, stop=True),
                     waits=[("qcs", 2), ("qbe", max(0, 2 * tb - 1)),
                            ("pbe", 16 if "proj" in phases else 0),
                            ("pt1", 64 if "proj" in phases else 0)],
                     incs=[("qbc", 1)])
                P.do("tensor", lambda t, tb=tb: t.matmul(
                     bank(4), ones1[:], sqrow[0:1, tb * TBW:(tb + 1) * TBW], start=True, stop=True),
                     waits=[("qbe", max(0, 2 * tb))],
                     incs=[("qbc", 1)])
                P.do("vector", lambda v: v.tensor_copy(out=c0B[:], in_=bank(3)),
                     waits=[("qbc", 2 * tb + 1)], incs=[("qbe", 1)])
                P.do("vector", lambda v: v.tensor_copy(out=c1B[:], in_=bank(4)),
                     waits=[("qbc", 2 * tb + 2)], incs=[("qbe", 1)])
                for k in range(KT):
                    tsl = k % 2
                    P.do("vector", lambda v, pb=pb, k=k, tsl=tsl: v.tensor_mul(
                        out=tmpf[tsl][:], in0=xs[pb][:, k * TBW:(k + 1) * TBW], in1=c1B[:]),
                        waits=[("xl", xl0 + 16 * (tb + 1)),
                               ("qbe", 2 * tb + 2), ("xqd", max(0, 8 * tb + k - 1))],
                        incs=[("qt1", 1)])
                    P.do("vector", lambda v, tsl=tsl: v.tensor_add(
                        out=tmpf[tsl][:], in0=tmpf[tsl][:], in1=c0B[:]),
                        waits=[("qt1", 8 * tb + k + 1)], incs=[("qt2", 1)])
                    P.do("vector", lambda v, tb=tb, k=k, tsl=tsl: v.tensor_scalar(
                        out=xqv[k][:, tb * TBW:(tb + 1) * TBW], in0=tmpf[tsl][:],
                        scalar1=C_ROUND, scalar2=-C_ROUND, op0=ALU.add, op1=ALU.add),
                        waits=[("pmm", nproj_mm), ("qt2", 8 * tb + k + 1)],
                        incs=[("xqd", 1)])

        P.paranoid = "attn" in paranoid
        if "attn" in phases:
            for b in range(2):
                for hp in range(HP):
                    g = b * HP + hp
                    sl = g % 2
                    lw = [("amst", max(0, 128 * (g - 1))), ("avm", max(0, 128 * (g - 1))),
                          ("psp", 16 * 128 if "proj" in phases else 0),
                          ("asp", 512 * b)]
                    P.do("sync", lambda s, b=b, hp=hp, sl=sl: s.dma_start(
                        out=qhp[sl][:], in_=qs[hp][:, b * SEQ:(b + 1) * SEQ]),
                        waits=lw, incs=[("al", 16)])
                    P.do("sync", lambda s, b=b, hp=hp, sl=sl: s.dma_start(
                        out=khp[sl][:], in_=kvs[hp][:, b * SEQ:(b + 1) * SEQ]),
                        incs=[("al", 16)])
                    P.do("sync", lambda s, b=b, hp=hp, sl=sl: s.dma_start(
                        out=tokhp[sl].rearrange("p c d -> p c d"),
                        in_=kts[b, hp].rearrange("c p d -> p c d"),),
                        waits=[("pts", 16 * 64 if "proj" in phases else 0)],
                        incs=[("al", 16)])
                    for qq in range(QQ):
                        uq = g * QQ + qq
                        avp = uq % 2
                        for tc in range(TC):
                            u = uq * TC + tc
                            stp = tc % 2
                            P.do("tensor", lambda t, sl=sl, qq=qq, tc=tc, stp=stp: t.matmul(
                                bank(2 * stp), khp[sl][0:64, tc * 128:(tc + 1) * 128],
                                qhp[sl][0:64, qq * TBW:(qq + 1) * TBW],
                                start=True, stop=True, tile_position=(0, 0)),
                                waits=[("al", 48 * (g + 1)), ("aev", max(0, u - 1)),
                                       ("qbe", 16 if (u == 0 and "xq" in phases) else 0),
                                       ("a1ch", 1 if (u == 0 and "proj" in phases) else 0),
                                       ("ape", 32 * b if (hp == 0 and qq == 0 and tc == 0) else 0)],
                                incs=[("amst", 1)])
                            P.do("tensor", lambda t, sl=sl, qq=qq, tc=tc, stp=stp: t.matmul(
                                bank(2 * stp + 1), khp[sl][64:128, tc * 128:(tc + 1) * 128],
                                qhp[sl][64:128, qq * TBW:(qq + 1) * TBW],
                                start=True, stop=True, tile_position=(64, 0)),
                                incs=[("amst", 1)])
                            if tc % 2 == 0:
                                P.do("scalar", lambda a, stp=stp, bsl=tc % 2: a.activation(
                                    out=stbuf[bsl][:], in_=PSA[:, stp * 1024: stp * 1024 + 1024],
                                    func=AF.Relu),
                                    waits=[("amst", 2 * (u + 1)), ("avm", max(0, 2 * u - 2))],
                                    incs=[("aev", 1)])
                            else:
                                P.do("vector", lambda v, stp=stp, bsl=tc % 2: v.tensor_scalar_max(
                                    out=stbuf[bsl][:], in0=PSA[:, stp * 1024: stp * 1024 + 1024],
                                    scalar1=0.0),
                                    waits=[("amst", 2 * (u + 1)), ("avm", max(0, 2 * u - 2))],
                                    incs=[("aev", 1)])
                            avt = PSA[:, 2048:2560] if avp == 0 else PSB[:, 0:512]
                            P.do("tensor", lambda t, sl=sl, tc=tc, bsl=tc % 2, avt=avt: t.matmul(
                                avt[0:64, :],
                                tokhp[sl][:, tc, 0:64], stbuf[bsl][:, 0:TBW],
                                start=(tc == 0), stop=(tc == TC - 1), tile_position=(0, 0)),
                                waits=[("aev", u + 1), ("aoe", max(0, uq - 1))],
                                incs=[("avm", 1)])
                            P.do("tensor", lambda t, sl=sl, tc=tc, bsl=tc % 2, avt=avt: t.matmul(
                                avt[64:128, :],
                                tokhp[sl][:, tc, 64:128], stbuf[bsl][:, TBW:2 * TBW],
                                start=(tc == 0), stop=(tc == TC - 1), tile_position=(0, 64)),
                                incs=[("avm", 1)])
                        eng = "scalar" if uq % 2 == 0 else "vector"
                        avt = PSA[:, 2048:2560] if avp == 0 else PSB[:, 0:512]
                        emit = (lambda e, hp=hp, qq=qq, avt=avt, eng=eng:
                                (e.copy if eng == "scalar" else e.tensor_copy)(
                                    out=outTT[hp][:, qq * TBW:(qq + 1) * TBW], in_=avt))
                        P.do(eng, emit,
                             waits=[("avm", 32 * (uq + 1)), ("apm", 256 * b)],
                             incs=[("aoe", 1)])
                for dt in range(8):
                    gd = b * 8 + dt
                    wsl = gd % 2
                    P.do("sync", lambda s, dt=dt, wsl=wsl: s.dma_start(
                        out=wst[wsl].rearrange("p f d -> p f d"),
                        in_=wouts[dt].rearrange("f p d -> p f d")),
                        waits=[("wos", 128), ("apm", max(0, 32 * (gd - 1)))],
                        incs=[("awl", 16)])
                    for qq in range(QQ):
                        gq = gd * QQ + qq
                        obk = gq % 2
                        for f in range(FT):
                            P.do("tensor", lambda t, f=f, qq=qq, wsl=wsl, obk=obk: t.matmul(
                                bank(obk), wst[wsl][:, f, :], outTT[f][:, qq * TBW:(qq + 1) * TBW],
                                start=(f == 0), stop=(f == FT - 1)),
                                waits=[("awl", 16 * (gd + 1)), ("aoe", 32 * (b + 1)),
                                       ("ape", max(0, gq - 1))],
                                incs=[("apm", 1)])
                        asl = gq % 2
                        P.do("scalar", lambda a, dt=dt, asl=asl, obk=obk: a.activation(
                            out=aos[asl][:], in_=bank(obk), func=AF.Identity, bias=boutT[:, dt:dt + 1]),
                            waits=[("apm", 8 * (gq + 1)), ("asp", max(0, 16 * (gq - 1)))],
                            incs=[("ape", 1)])
                        P.do("sync", lambda s, b=b, dt=dt, qq=qq, asl=asl: s.dma_start(
                            out=att_s[b, dt][:, qq * TBW:(qq + 1) * TBW], in_=aos[asl][:]),
                            waits=[("ape", gq + 1)], incs=[("asp", 16)])

        P.paranoid = "ffn1" in paranoid
        if "ffn1" in phases:
            post_attn = [("asp", 1024 if "attn" in phases else 0)]
            for k in range(KT):
                sl = k % 2
                P.do("sync", lambda s, k=k, sl=sl: s.dma_start(
                    out=wls[sl][:], in_=w1T[k * 128:(k + 1) * 128, :]),
                    waits=[("f1wr", max(0, k - 1))] + post_attn,
                    incs=[("f1wl", 16)])
                P.do("vector", lambda v, sl=sl: v.tensor_reduce(
                    out=tmpc[:], in_=wls[sl][:], axis=AX.X, op=ALU.max, apply_absolute_value=True),
                    waits=[("f1wl", 16 * (k + 1)), ("f1wr", max(0, k))], incs=[("f1wr0", 1)])
                P.do("vector", lambda v: v.tensor_max(out=w1acc[:], in0=w1acc[:], in1=tmpc[:]),
                     waits=[("f1wr0", k + 1)], incs=[("f1wr", 1)])
            pmax_chain("w1", w1acc[:], 4, rw1B, sw1B,
                       [("f1wr", KT), ("apm", 512 if "attn" in phases else 0),
                        ("aoe", 64 if "attn" in phases else 0)])
            P.do("vector", lambda v: v.tensor_mul(out=s1B[:], in0=sw1B[:], in1=sxB[:]),
                 waits=[("w1ch", 5), ("a1ch", 5 if "proj" in phases else 0)], incs=[("f1sc", 1)])
            P.do("vector", lambda v: v.tensor_copy(out=mrow[0:1, 5:6], in_=s1B[0:1, :]), incs=[("dsc", 1)])
            P.do("vector", lambda v: v.tensor_copy(out=mrow[0:1, 6:7], in_=rB[0:1, :]), incs=[("dsc", 1)])
            P.do("vector", lambda v: v.tensor_copy(out=mrow[0:1, 7:8], in_=sxB[0:1, :]), incs=[("dsc", 1)])
            P.do("vector", lambda v: v.tensor_copy(out=mrow[0:1, 3:4], in_=rw1B[0:1, :]), incs=[("dsc", 1)])
            P.do("vector", lambda v: v.tensor_scalar_mul(out=b1s[:], in0=b1T[:], scalar1=s1B[:]),
                 waits=[("f1sc", 1), ("bte", 48)], incs=[("f1be", 3)])
            for k in range(KT):
                sl = k % 2
                P.do("sync", lambda s, k=k, sl=sl: s.dma_start(
                    out=wls[sl][:], in_=w1T[k * 128:(k + 1) * 128, :]),
                    waits=[("f1qz", max(0, k - 1)), ("f1be", 3),
                           ("qbc", 16 if "xq" in phases else 0)],
                    incs=[("f1w2", 16)])
                P.do("scalar", lambda a, sl=sl: a.activation(
                    out=wls[sl][:], in_=wls[sl][:], func=AF.Copy, scale=rw1B[:], bias=C_ROUND),
                    waits=[("f1w2", 16 * (k + 1))], incs=[("f1qa", 1)])
                P.do("vector", lambda v, k=k, sl=sl: v.tensor_scalar_add(
                    out=w1q[k][:], in0=wls[sl][:], scalar1=-C_ROUND),
                    waits=[("f1qa", k + 1)], incs=[("f1qz", 1)])
            for tb in range(NTB):
                for jt in range(JT):
                    seq = tb * JT + jt
                    bk = seq % 5
                    for k in range(KT):
                        P.do("tensor", lambda t, tb=tb, jt=jt, k=k, bk=bk: t.matmul(
                            bank(bk), w1q[k][:, jt * 128:(jt + 1) * 128],
                            xqv[k][:, tb * TBW:(tb + 1) * TBW],
                            start=(k == 0), stop=(k == KT - 1)),
                            waits=[("f1qz", KT), ("xqd", 64),
                                   ("f1e", max(0, seq - 4))],
                            incs=[("f1m", 1)])
                    ssl = seq % 4
                    P.do("scalar", lambda a, bk=bk, ssl=ssl, jt=jt: a.activation(
                        out=hst[ssl][:], in_=bank(bk), func=AF.Gelu,
                        scale=s1B[:], bias=b1s[:, jt:jt + 1]),
                        waits=[("f1m", 8 * (seq + 1)), ("f1s", max(0, 16 * (seq - 3))),
                               ("f1r", max(0, 2 * (seq - 3)))],
                        incs=[("f1e", 1)])
                    P.do("sync", lambda s, jt=jt, tb=tb, ssl=ssl: s.dma_start(
                        out=hsp[jt][:, tb * TBW:(tb + 1) * TBW], in_=hst[ssl][:]),
                        waits=[("f1e", seq + 1)], incs=[("f1s", 16)])
                    P.do("vector", lambda v, ssl=ssl: v.tensor_reduce(
                        out=tmpc[:], in_=hst[ssl][:], axis=AX.X, op=ALU.max,
                        apply_absolute_value=True),
                        waits=[("f1e", seq + 1), ("f1r", max(0, 2 * seq))], incs=[("f1r0", 1)])
                    P.do("vector", lambda v: v.tensor_max(out=hacc[:], in0=hacc[:], in1=tmpc[:]),
                         waits=[("f1r0", seq + 1)], incs=[("f1r", 2)])
            P.do("vector", lambda v: v.tensor_copy(out=ccs[:, 0:1], in_=hacc[:]),
                 waits=[("f1r", 2 * NTB * JT)], incs=[("ghx", 1)])
            P.do("sync", lambda s: s.dma_start(out=ccin2[:], in_=ccs[:]),
                 waits=[("ghx", 1)], incs=[("gst2", 16)])
            P.do("gpsimd", lambda g: g.collective_compute(
                "AllReduce", ALU.max, replica_groups=[list(range(8))],
                ins=[ccin2[:]], outs=[ccout2[:]]),
                waits=[("gst2", 16)], incs=[("cc2", None)])
            P.do("sync", lambda s: s.dma_start(out=ccr[:], in_=ccout2[:]),
                 waits=[("cc2", 1)], incs=[("gld2", 16)])
            pmax_chain("hh", ccr[:, 0:1], 8, rhB, shB,
                       [("gld2", 16), ("f1e", 256)])
            if "dxq" in dbg_out:
                for k in range(KT):
                    P.do("sync", lambda s, k=k: s.dma_start(out=dxq[k], in_=xqv[k][:]),
                         waits=[("xqd", 64), ("f1qz", KT)], incs=[("ddmp", 16)])
                    P.do("sync", lambda s, k=k: s.dma_start(out=dw1q[k], in_=w1q[k][:]),
                         incs=[("ddmp", 16)])
                P.do("gpsimd", lambda g: g.dma_start(out=mxs[:], in_=mrow[:]),
                     waits=[("gld2", 16), ("dsc", 4)], incs=[("ddmp", 16)])
                P.do("sync", lambda s: None, waits=[("ddmp", P.n("ddmp"))])

        P.paranoid = "ffn2" in paranoid
        if "ffn2" in phases:
            for j in range(JT):
                sl = j % 2
                P.do("sync", lambda s, j=j, sl=sl: s.dma_start(
                    out=w2ls[sl][:], in_=w2T[j * 128:(j + 1) * 128, :]),
                    waits=[("f2wr", max(0, j - 1)),
                           ("apm", 512 if "attn" in phases else 0)],
                    incs=[("f2wl", 16)])
                P.do("vector", lambda v, sl=sl: v.tensor_reduce(
                    out=tmpc[:], in_=w2ls[sl][:], axis=AX.X, op=ALU.max, apply_absolute_value=True),
                    waits=[("f2wl", 16 * (j + 1)), ("f2wr", max(0, j))], incs=[("f2wr0", 1)])
                P.do("vector", lambda v: v.tensor_max(out=w2acc[:], in0=w2acc[:], in1=tmpc[:]),
                     waits=[("f2wr0", j + 1)], incs=[("f2wr", 1)])
            pmax_chain("w2", w2acc[:], 12, rw2B, sw2B,
                       [("f2wr", JT), ("f1e", 256 if "ffn1" in phases else 0),
                        ("hhch", 5 if "ffn1" in phases else 0)])
            P.do("vector", lambda v: v.tensor_mul(out=s2B[:], in0=sw2B[:], in1=shB[:]),
                 waits=[("w2ch", 5), ("hhch", 5 if "ffn1" in phases else 0)], incs=[("f2sc", 1)])
            P.do("vector", lambda v: v.tensor_scalar_mul(out=b2s[:], in0=b2T[:], scalar1=s2B[:]),
                 waits=[("f2sc", 1), ("bte", 48)], incs=[("f2be", 4)])
            for j in range(JT):
                sl = j % 2
                P.do("sync", lambda s, j=j, sl=sl: s.dma_start(
                    out=w2ls[sl][:], in_=w2T[j * 128:(j + 1) * 128, :]),
                    waits=[("f2qz", max(0, j - 1)), ("f2be", 4)],
                    incs=[("f2w2", 16)])
                P.do("scalar", lambda a, sl=sl: a.activation(
                    out=w2ls[sl][:], in_=w2ls[sl][:], func=AF.Copy, scale=rw2B[:], bias=C_ROUND),
                    waits=[("f2w2", 16 * (j + 1))], incs=[("f2qa", 1)])
                P.do("vector", lambda v, j=j, sl=sl: v.tensor_scalar_add(
                    out=w2q[j][:], in0=w2ls[sl][:], scalar1=-C_ROUND),
                    waits=[("f2qa", j + 1)], incs=[("f2qz", 1)])
            hspr = hsp.rearrange("(g c) p w -> g c p w", c=4)
            for tb in range(NTB):
                b = tb // 4
                qcol = (tb % 4) * TBW
                for jg in range(8):
                    u = tb * 8 + jg
                    sl = u % 2
                    P.do("sync", lambda s, tb=tb, jg=jg, sl=sl: s.dma_start(
                        out=hld[sl].rearrange("p (c w) -> p c w", w=TBW),
                        in_=hspr[jg][:, :, tb * TBW:(tb + 1) * TBW].rearrange("c p w -> p c w")),
                        waits=[("hqd", max(0, u - 1)),
                               ("f1s", 16 * NTB * JT if "ffn1" in phases else 0)],
                        incs=[("hxl", 16)])
                    P.do("scalar", lambda a, sl=sl: a.activation(
                        out=hld[sl][:], in_=hld[sl][:], func=AF.Copy, scale=rhB[:], bias=C_ROUND),
                        waits=[("hxl", 16 * (u + 1)), ("f2be", 4)], incs=[("hqa", 1)])
                    P.do("vector", lambda v, jg=jg, sl=sl: v.tensor_scalar_add(
                        out=hqv[:, jg * 2048:(jg + 1) * 2048], in0=hld[sl][:], scalar1=-C_ROUND),
                        waits=[("hqa", u + 1), ("f2m", max(0, 256 * (tb - 1)))],
                        incs=[("hqd", 1)])
                P.do("sync", lambda s, b=b, qcol=qcol: s.dma_start(
                    out=atts[:],
                    in_=att_s[b][:, :, qcol:qcol + TBW].rearrange("dt p w -> p dt w")),
                    waits=[("asp", 1024 if "attn" in phases else 0),
                           ("f2qz", JT),
                           ("yev", max(0, 16 * (tb - 1)))],
                    incs=[("f2al", 16)])
                for dt in range(8):
                    seq = tb * 8 + dt
                    bk = seq % 4
                    xsl_sl = seq % 2
                    P.do("sync", lambda s, dt=dt, tb=tb, xsl_sl=xsl_sl: s.dma_start(
                        out=xsl[xsl_sl][:], in_=xT[dt * 128:(dt + 1) * 128, tb * TBW:(tb + 1) * TBW]),
                        waits=[("yev", max(0, 2 * seq - 2))],
                        incs=[("f2xl", 16)])
                    for jt in range(JT):
                        P.do("tensor", lambda t, jt=jt, dt=dt, bk=bk: t.matmul(
                            bank(bk), w2q[jt][:, dt * 128:(dt + 1) * 128],
                            hqv[:, jt * TBW:(jt + 1) * TBW],
                            start=(jt == 0), stop=(jt == JT - 1)),
                            waits=[("hqd", 8 * (tb + 1)), ("f2qz", JT),
                                   ("f2e", max(0, seq - 3))],
                            incs=[("f2m", 1)])
                    ysl = seq % 2
                    P.do("scalar", lambda a, bk=bk, dt=dt, ysl=ysl: a.activation(
                        out=yst[ysl][:], in_=bank(bk), func=AF.Identity, scale=s2B[:],
                        bias=b2s[:, dt:dt + 1]),
                        waits=[("f2m", 32 * (seq + 1)), ("yw", max(0, 16 * (seq - 1)))],
                        incs=[("f2e", 1)])
                    P.do("vector", lambda v, dt=dt, ysl=ysl: v.tensor_add(
                        out=yst[ysl][:], in0=yst[ysl][:], in1=atts[:, dt, :]),
                        waits=[("f2e", seq + 1), ("f2al", 16 * (tb + 1))],
                        incs=[("yev", 1)])
                    P.do("vector", lambda v, ysl=ysl, xsl_sl=xsl_sl: v.tensor_add(
                        out=yst[ysl][:], in0=yst[ysl][:], in1=xsl[xsl_sl][:]),
                        waits=[("f2xl", 16 * (seq + 1)), ("yev", 2 * seq + 1)],
                        incs=[("yev", 1)])
                    P.do("sync", lambda s, dt=dt, tb=tb, ysl=ysl: s.dma_start(
                        out=yT[dt * 128:(dt + 1) * 128, tb * TBW:(tb + 1) * TBW], in_=yst[ysl][:]),
                        waits=[("yev", 2 * (seq + 1))], incs=[("yw", 16)])
            P.do("sync", lambda s: None, waits=[("yw", 16 * 64)])
        else:
            for snm in ("psp", "pts", "asp", "f1s", "wos"):
                if P.n(snm):
                    P.do("sync", lambda e, snm=snm: None, waits=[(snm, P.n(snm))])
            if "proj" in phases:
                P.do("gpsimd", lambda g: g.dma_start(out=mxs[:], in_=mrow[:]),
                     waits=[("a1ch", 5), ("a1d", 4)], incs=[("dmx", 16)])
                P.do("sync", lambda s: None, waits=[("dmx", 16)])
            if "xq" in phases and "dxq" in dbg_out:
                for k in range(KT):
                    P.do("sync", lambda s, k=k: s.dma_start(out=dxq[k], in_=xqv[k][:]),
                         waits=[("xqd", 64)], incs=[("ddmp", 16)])
                P.do("sync", lambda s: None, waits=[("ddmp", P.n("ddmp"))])

        with nc.Block() as block:
            P.replay(block)

    return nc


def make_inputs(x, Wq, Wkv, Wout, bout, w1, b1, w2, b2):
    import ml_dtypes
    w1T = np.ascontiguousarray(w1.T)
    w2T = np.ascontiguousarray(w2.T)
    b1m = np.ascontiguousarray(b1.reshape(32, 128).T)
    b2m = np.ascontiguousarray(b2.reshape(8, 128).T)
    boutm = np.ascontiguousarray(bout.reshape(8, 128).T)
    identF = np.eye(128, dtype=np.float32)
    identB = np.eye(128, dtype=ml_dtypes.bfloat16)
    in_maps = []
    for c in range(8):
        xTc = np.ascontiguousarray(x[2 * c:2 * c + 2].transpose(2, 0, 1).reshape(D, T))
        in_maps.append({
            "xT": xTc, "Wq": Wq, "Wkv": Wkv, "Wout": Wout,
            "w1T": w1T, "w2T": w2T, "b1m": b1m, "b2m": b2m, "boutm": boutm,
            "identF": identF, "identB": identB,
        })
    return in_maps


def kernel(L0_attn_hidden_states, L0_attn_Wq, L0_attn_Wkv, L0_attn_Wout, L0_attn_bout,
           L0_ffn_w1, L0_ffn_b1, L0_ffn_w2, L0_ffn_b2):
    x = np.asarray(L0_attn_hidden_states, dtype=np.float32)
    Wq = np.ascontiguousarray(np.asarray(L0_attn_Wq, dtype=np.float32))
    Wkv = np.ascontiguousarray(np.asarray(L0_attn_Wkv, dtype=np.float32))
    Wout = np.ascontiguousarray(np.asarray(L0_attn_Wout, dtype=np.float32))
    bout = np.asarray(L0_attn_bout, dtype=np.float32)
    w1 = np.asarray(L0_ffn_w1, dtype=np.float32)
    b1 = np.asarray(L0_ffn_b1, dtype=np.float32)
    w2 = np.asarray(L0_ffn_w2, dtype=np.float32)
    b2 = np.asarray(L0_ffn_b2, dtype=np.float32)

    nc = build_nc()
    in_maps = make_inputs(x, Wq, Wkv, Wout, bout, w1, b1, w2, b2)
    res = run_bass_kernel_spmd(nc, in_maps, core_ids=list(range(8)))
    y = np.empty((16, 2048, 1024), dtype=np.float32)
    for c in range(8):
        yTc = np.asarray(res.results[c]["yT"])
        y[2 * c:2 * c + 2] = yTc.reshape(D, 2, SEQ).transpose(1, 2, 0)
    return y


# version 02
# speedup vs baseline: 1.1339x; 1.1339x over previous
import numpy as np
from contextlib import ExitStack

import concourse.bass as bass
import concourse.mybir as mybir
from concourse.bass_utils import run_bass_kernel_spmd

F32 = mybir.dt.float32
BF16 = mybir.dt.bfloat16
AF = mybir.ActivationFunctionType
ALU = mybir.AluOpType
AX = mybir.AxisListType

D = 1024
T = 4096
SEQ = 2048
J = 4096
TBW = 512
NTB = 8
KT = 8
FT = 8
JT = 32
HP = 8
QQ = 4
TC = 16
C_ROUND = float(np.float32(3 << 22))
QP = 127.0
MIN_RANGE = 1e-6
LN_EPS = 1e-5

KB = 1024
S0, S1, S2, S3 = 0, 16 * KB, 32 * KB, 48 * KB
XQR = 64 * KB
WQR = 128 * KB
ARENA = 192 * KB


class Prog:
    def __init__(self, nc, ctx, paranoid=False):
        self.nc = nc
        self.ctx = ctx
        self.ops = []
        self.sems = {}
        self.counts = {}
        self.paranoid = paranoid

    def sem(self, name):
        if name not in self.sems:
            self.sems[name] = self.ctx.enter_context(self.nc.semaphore(name))
            self.counts[name] = 0
        return self.sems[name]

    def n(self, name):
        return self.counts.get(name, 0)

    def do(self, engine, emit, waits=(), incs=()):
        if self.paranoid:
            waits = list(waits) + [(s, v) for s, v in self.counts.items()]
        waits = [(s, v) for (s, v) in waits if v > 0]
        wd = {}
        for s, v in waits:
            wd[s] = max(wd.get(s, 0), v)
        waits = list(wd.items())
        for s, _ in waits:
            self.sem(s)
        assert len(incs) <= 1
        for s, k in incs:
            self.sem(s)
            self.counts[s] += (k if k is not None else 1)
        self.ops.append((engine, emit, waits, list(incs)))

    def replay(self, block):
        def mk(eng_name):
            def body(eng):
                hi = {}
                for engine, emit, waits, incs in self.ops:
                    if engine != eng_name:
                        continue
                    for s, v in waits:
                        if v > hi.get(s, 0):
                            eng.wait_ge(self.sems[s], v)
                            hi[s] = v
                    ins = emit(eng)
                    for s, k in incs:
                        assert ins is not None
                        if k is None:
                            ins.then_inc(self.sems[s])
                        else:
                            ins.then_inc(self.sems[s], k)
            return body
        block.sync(mk("sync"))
        block.tensor(mk("tensor"))
        block.scalar(mk("scalar"))
        block.vector(mk("vector"))
        block.gpsimd(mk("gpsimd"))


def build_nc(phases=("stats", "proj", "xq", "attn", "ffn1", "ffn2"), dbg_out=(), paranoid=()):
    nc = bass.Bass("TRN2", target_bir_lowering=False)

    xT = nc.dram_tensor("xT", [D, T], F32, kind="ExternalInput")
    Wq = nc.dram_tensor("Wq", [D, D], F32, kind="ExternalInput")
    Wkv = nc.dram_tensor("Wkv", [D, D], F32, kind="ExternalInput")
    Wout = nc.dram_tensor("Wout", [D, D], F32, kind="ExternalInput")
    w1T = nc.dram_tensor("w1T", [D, J], F32, kind="ExternalInput")
    w2T = nc.dram_tensor("w2T", [J, D], F32, kind="ExternalInput")
    b1m = nc.dram_tensor("b1m", [128, 32], F32, kind="ExternalInput")
    b2m = nc.dram_tensor("b2m", [128, 8], F32, kind="ExternalInput")
    boutm = nc.dram_tensor("boutm", [128, 8], F32, kind="ExternalInput")
    identF = nc.dram_tensor("identF", [128, 128], F32, kind="ExternalInput")
    identB = nc.dram_tensor("identB", [128, 128], BF16, kind="ExternalInput")
    yT = nc.dram_tensor("yT", [D, T], F32, kind="ExternalOutput")

    def scratch(name, shape, dtype):
        kind = "ExternalOutput" if name in dbg_out else "Internal"
        return nc.dram_tensor(name, shape, dtype, kind=kind)

    qs = scratch("qs", [FT, 128, T], BF16)
    kvs = scratch("kvs", [FT, 128, T], BF16)
    kts = scratch("kts", [2, HP, TC, 128, 128], BF16)
    wouts = scratch("wouts", [8, 8, 128, 128], BF16)
    hsp = scratch("hsp", [JT, 128, T], F32)
    att_s = scratch("att_s", [2, 8, 128, SEQ], BF16)
    mxs = scratch("mxs", [1, 16], F32)
    dxq = scratch("dxq", [KT, 128, T], BF16)
    dmaxa = scratch("dmaxa", [128, 1], F32)
    dw1q = scratch("dw1q", [KT, 128, J], BF16)
    ccin1 = nc.dram_tensor("ccin1", [128, 16], F32)
    ccout1 = nc.dram_tensor("ccout1", [128, 16], F32)
    dbgcc = scratch("dbgcc", [128, 32], F32)
    ccin2 = nc.dram_tensor("ccin2", [128, 16], F32)
    ccout2 = nc.dram_tensor("ccout2", [128, 16], F32)

    with ExitStack() as ctx:
        arena = ctx.enter_context(nc.sbuf_tensor("arena", [128, ARENA], mybir.dt.uint8))
        base = nc.lookup_mloc(arena).addr

        def at(name, shape, dtype, off):
            return nc.alloc_sbuf_tensor_at(name, shape, dtype, offset=base + off)

        def sbt(name, shape, dtype):
            return ctx.enter_context(nc.sbuf_tensor(name, shape, dtype))

        ones1 = sbt("ones1", [1, 128], F32)
        onesC = sbt("onesC", [128, 1], F32)
        idF = sbt("idF", [128, 128], F32)
        idB = sbt("idB", [128, 128], BF16)
        b1T = sbt("b1T", [128, 32], F32)
        b2T = sbt("b2T", [128, 8], F32)
        boutT = sbt("boutT", [128, 8], F32)
        b1s = sbt("b1s", [128, 32], F32)
        b2s = sbt("b2s", [128, 8], F32)
        maxacc = sbt("maxacc", [128, 1], F32)
        w1acc = sbt("w1acc", [128, 1], F32)
        w2acc = sbt("w2acc", [128, 1], F32)
        hacc = sbt("hacc", [128, 1], F32)
        tmpc = sbt("tmpc", [128, 1], F32)
        epsB = sbt("epsB", [128, 1], F32)
        mrow = sbt("mrow", [1, 16], F32)
        s1B = sbt("s1B", [128, 1], F32)
        s2B = sbt("s2B", [128, 1], F32)
        rw1B = sbt("rw1B", [128, 1], F32)
        rw2B = sbt("rw2B", [128, 1], F32)
        rhB = sbt("rhB", [128, 1], F32)
        ccs = sbt("ccs", [128, 16], F32)
        rowb = sbt("rowb", [1, 128], F32)
        mtmp = sbt("mtmp", [128, 1], F32)
        mcl = sbt("mcl", [128, 1], F32)
        mrc = sbt("mrc", [128, 1], F32)
        rB = sbt("rB", [128, 1], F32)
        sxB = sbt("sxB", [128, 1], F32)
        shB = sbt("shB", [128, 1], F32)
        sw1B = sbt("sw1B", [128, 1], F32)
        sw2B = sbt("sw2B", [128, 1], F32)
        ccr = sbt("ccr", [128, 16], F32)
        stbuf = [sbt(f"stbuf{i}", [128, 1024], BF16) for i in range(2)]
        aos = [sbt(f"aos{i}", [128, TBW], BF16) for i in range(2)]
        ktt = [sbt(f"ktt{i}", [128, TBW], BF16) for i in range(2)]

        PSA = ctx.enter_context(nc.psum_tensor("PSA", [128, 2560], F32))
        PSB = ctx.enter_context(nc.psum_tensor("PSB", [128, 512], F32))
        PST = ctx.enter_context(nc.psum_tensor("PST", [128, 2048], BF16))

        def bank(i):
            return PSA[:, i * 512:(i + 1) * 512]

        def pmax_chain(pref, src, mslot, rB_t, sB_t, init_waits):
            P.do("tensor", lambda t: t.transpose(PSB[0:1, 128:256], src, idF[:]),
                 waits=init_waits)
            P.do("tensor", lambda t: t.matmul(
                 PSB[0:1, 0:1], ones1[0:1, 0:1], ones1[0:1, 0:1], start=True, stop=True),
                 incs=[(pref + "fx", 1)])
            P.do("vector", lambda v: v.tensor_copy(out=rowb[:], in_=PSB[0:1, 128:256]),
                 waits=[(pref + "fx", 1)], incs=[(pref + "rd", 1)])
            P.do("vector", lambda v, mslot=mslot: v.reduce_max(
                 out=mrow[0:1, mslot:mslot + 1], in_=rowb[:], axis=AX.X),
                 waits=[(pref + "rd", 1)], incs=[(pref + "rd", 1)])
            P.do("tensor", lambda t, mslot=mslot: t.matmul(
                 PSB[:, 256:257], ones1[:], mrow[0:1, mslot:mslot + 1], start=True, stop=True),
                 waits=[(pref + "rd", 2)])
            P.do("tensor", lambda t: t.matmul(
                 PSB[0:1, 384:385], ones1[0:1, 0:1], ones1[0:1, 0:1], start=True, stop=True),
                 incs=[(pref + "f2", 1)])
            P.do("vector", lambda v: v.tensor_copy(out=mtmp[:], in_=PSB[:, 256:257]),
                 waits=[(pref + "f2", 1)], incs=[(pref + "ch", 1)])
            P.do("vector", lambda v: v.tensor_scalar_max(out=mcl[:], in0=mtmp[:], scalar1=MIN_RANGE),
                 waits=[(pref + "ch", 1)], incs=[(pref + "ch", 1)])
            P.do("vector", lambda v: v.reciprocal(out=mrc[:], in_=mcl[:]),
                 waits=[(pref + "ch", 2)], incs=[(pref + "ch", 1)])
            P.do("vector", lambda v, rB_t=rB_t: v.tensor_scalar_mul(out=rB_t[:], in0=mrc[:], scalar1=QP),
                 waits=[(pref + "ch", 3)], incs=[(pref + "ch", 1)])
            P.do("vector", lambda v, sB_t=sB_t: v.tensor_scalar_mul(out=sB_t[:], in0=mcl[:], scalar1=1.0 / QP),
                 waits=[(pref + "ch", 4)], incs=[(pref + "ch", 1)])

        xs = [at("xs0", [128, T], F32, S0), at("xs1", [128, T], F32, S1)]
        sumrow = at("sumrow", [1, T], F32, WQR)
        sqrow = at("sqrow", [1, T], F32, WQR + 16 * KB)
        vrow = at("vrow", [1, T], F32, WQR + 32 * KB)
        sqb = [at("sqb0", [128, TBW], F32, XQR + 32 * KB),
               at("sqb1", [128, TBW], F32, XQR + 34 * KB)]
        wfs = [at("wfs0", [128, D], F32, S2), at("wfs1", [128, D], F32, S2 + 4 * KB)]
        wbs = at("wbs", [128, D], BF16, S2 + 8 * KB)
        xnb = [at("xnb0", [128, KT, TBW], BF16, S2),
               at("xnb1", [128, KT, TBW], BF16, S2 + 8 * KB)]
        c0B = at("c0B", [128, TBW], F32, S3)
        c1B = at("c1B", [128, TBW], F32, S3 + 2 * KB)
        tmpf = [at("tmpf0", [128, TBW], F32, S3 + 8 * KB),
                at("tmpf1", [128, TBW], F32, S3 + 10 * KB)]
        qst = [at("qst0", [128, TBW], BF16, S3 + 12 * KB),
               at("qst1", [128, TBW], BF16, S3 + 13 * KB)]
        kvt = [at("kvt0", [128, TBW], BF16, S3 + 14 * KB),
               at("kvt1", [128, TBW], BF16, S3 + 15 * KB)]
        xqv = [at(f"xq{k}", [128, T], BF16, XQR + k * 8 * KB) for k in range(KT)]
        Wqb = [at(f"Wqb{k}", [128, D], BF16, XQR + k * 2 * KB) for k in range(KT)]
        Wkvb = [at(f"Wkvb{k}", [128, D], BF16, XQR + 16 * KB + k * 2 * KB) for k in range(KT)]
        qhp = [at("qhp0", [128, SEQ], BF16, S0), at("qhp1", [128, SEQ], BF16, S0 + 4 * KB)]
        khp = [at("khp0", [128, SEQ], BF16, S0 + 8 * KB), at("khp1", [128, SEQ], BF16, S0 + 12 * KB)]
        tokhp = [at("tok0", [128, TC, 128], BF16, S1), at("tok1", [128, TC, 128], BF16, S1 + 4 * KB)]
        outTT = [at(f"oTT{f}", [128, SEQ], BF16, S2 + f * 4 * KB) for f in range(FT)]
        wst = [at("wst0", [128, 8, 128], BF16, S1 + 8 * KB), at("wst1", [128, 8, 128], BF16, S1 + 10 * KB)]
        wls = [at("wls0", [128, J], F32, S0), at("wls1", [128, J], F32, S1)]
        w1q = [at(f"w1q{k}", [128, J], BF16, WQR + k * 8 * KB) for k in range(KT)]
        hst = [at(f"hst{i}", [128, TBW], F32, S2 + i * 2 * KB) for i in range(4)]
        hld = [at(f"hld{i}", [128, 2048], F32, S0 + i * 8 * KB) for i in range(2)]
        hqv = at("hq", [128, 16384], BF16, S1)
        w2q = [at(f"w2q{j}", [128, D], BF16, WQR + j * 2 * KB) for j in range(JT)]
        w2ls = [at("w2ls0", [128, D], F32, S3), at("w2ls1", [128, D], F32, S3 + 4 * KB)]
        atts = at("atts", [128, 8, TBW], BF16, S3)
        xsl = [at("xsl0", [128, TBW], F32, S3 + 8 * KB), at("xsl1", [128, TBW], F32, S3 + 10 * KB)]
        yst = [at("yst0", [128, TBW], F32, S3 + 12 * KB), at("yst1", [128, TBW], F32, S3 + 14 * KB)]

        P = Prog(nc, ctx, paranoid=False)
        paranoid = set(phases if paranoid is True else (paranoid or ()))
        xTr = xT.rearrange("(k p) (n w) -> n p k w", p=128, w=TBW)

        def x_load(buf, tb, waits):
            P.do("sync", lambda s, buf=buf, tb=tb: s.dma_start(
                out=xs[buf].rearrange("p (k w) -> p k w", w=TBW), in_=xTr[tb]),
                waits=waits, incs=[("xl", 16)])

        for t_ in (maxacc, w1acc, w2acc, hacc):
            P.do("vector", lambda v, t_=t_: v.memset(t_[:], 0.0))
        P.do("vector", lambda v: v.memset(epsB[:], LN_EPS), incs=[("cst", 1)])
        P.do("vector", lambda v: v.memset(ones1[:], 1.0))
        P.do("vector", lambda v: v.memset(onesC[:], 1.0))
        P.do("vector", lambda v: v.memset(ccs[:], 0.0), incs=[("cst", 1)])
        P.do("sync", lambda s: s.dma_start(out=idF[:], in_=identF[:]), incs=[("ini", 16)])
        P.do("sync", lambda s: s.dma_start(out=idB[:], in_=identB[:]), incs=[("ini", 16)])
        P.do("sync", lambda s: s.dma_start(out=b1T[:], in_=b1m[:]), incs=[("bte", 16)])
        P.do("sync", lambda s: s.dma_start(out=b2T[:], in_=b2m[:]), incs=[("bte", 16)])
        P.do("sync", lambda s: s.dma_start(out=boutT[:], in_=boutm[:]), incs=[("bte", 16)])
        for k in range(KT):
            for wsrc, wdst, nm in ((Wq, Wqb, "q"), (Wkv, Wkvb, "kv")):
                wi = P.n("wql") // 16
                P.do("sync", lambda s, k=k, wsrc=wsrc, wi=wi: s.dma_start(
                    out=wfs[wi % 2][:], in_=wsrc[k * 128:(k + 1) * 128, :]),
                    waits=[("wqc", max(0, wi - 1))], incs=[("wql", 16)])
                P.do("vector", lambda v, k=k, wdst=wdst, wi=wi: v.tensor_copy(
                    out=wdst[k][:], in_=wfs[wi % 2][:]),
                    waits=[("wql", 16 * (wi + 1))], incs=[("wqc", 1)])
        for k in range(KT):
            wi = P.n("wql") // 16
            P.do("sync", lambda s, k=k, wi=wi: s.dma_start(
                out=wfs[wi % 2][:], in_=Wout[k * 128:(k + 1) * 128, :]),
                waits=[("wqc", max(0, wi - 1)), ("wos", max(0, 16 * (k - 1)))],
                incs=[("wql", 16)])
            P.do("vector", lambda v, k=k, wi=wi: v.tensor_copy(
                out=wbs[:], in_=wfs[wi % 2][:]),
                waits=[("wql", 16 * (wi + 1)), ("wos", max(0, 16 * k))], incs=[("wqc", 1)])
            P.do("sync", lambda s, k=k: s.dma_start(
                out=wouts[:, k].rearrange("dt p d -> p dt d"),
                in_=wbs.rearrange("p (dt d) -> p dt d", d=128)),
                waits=[("wqc", P.n("wqc"))], incs=[("wos", 16)])

        P.paranoid = "stats" in paranoid
        if "stats" in phases:
            xl0 = P.n("xl")
            for tb in range(NTB):
                pb = tb % 2
                x_load(pb, tb, [("smm", max(0, 16 * (tb - 1))), ("sqd", max(0, 8 * (tb - 1)))])
                for k in range(KT):
                    sl = k % 2
                    P.do("scalar", lambda a, pb=pb, k=k, sl=sl: a.square(
                        out=sqb[sl][:], in_=xs[pb][:, k * TBW:(k + 1) * TBW]),
                        waits=[("xl", xl0 + 16 * (tb + 1)), ("smm", max(0, 16 * tb + 2 * k - 2))],
                        incs=[("sqd", 1)])
                    P.do("tensor", lambda t, tb=tb, pb=pb, k=k: t.matmul(
                        PSA[0:1, (tb % 2) * 512:(tb % 2) * 512 + 512],
                        onesC[:], xs[pb][:, k * TBW:(k + 1) * TBW],
                        start=(k == 0), stop=(k == KT - 1)),
                        waits=[("xl", xl0 + 16 * (tb + 1)), ("sev", max(0, 2 * (tb - 1)))],
                        incs=[("smm", 1)])
                    P.do("tensor", lambda t, tb=tb, k=k, sl=sl: t.matmul(
                        PSA[0:1, 1024 + (tb % 2) * 512:1024 + (tb % 2) * 512 + 512],
                        onesC[:], sqb[sl][:],
                        start=(k == 0), stop=(k == KT - 1)),
                        waits=[("sqd", 8 * tb + k + 1), ("sev", max(0, 2 * (tb - 1)))],
                        incs=[("smm", 1)])
                P.do("vector", lambda v, tb=tb: v.tensor_copy(
                    out=sumrow[0:1, tb * TBW:(tb + 1) * TBW],
                    in_=PSA[0:1, (tb % 2) * 512:(tb % 2) * 512 + 512]),
                    waits=[("smm", 16 * (tb + 1))], incs=[("sev", 1)])
                P.do("vector", lambda v, tb=tb: v.tensor_copy(
                    out=sqrow[0:1, tb * TBW:(tb + 1) * TBW],
                    in_=PSA[0:1, 1024 + (tb % 2) * 512:1024 + (tb % 2) * 512 + 512]),
                    waits=[("smm", 16 * (tb + 1))], incs=[("sev", 1)])
            P.do("vector", lambda v: v.tensor_scalar_mul(out=sumrow[:], in0=sumrow[:], scalar1=1.0 / D),
                 waits=[("sev", 16)], incs=[("rch", 1)])
            P.do("vector", lambda v: v.tensor_scalar_mul(out=sqrow[:], in0=sqrow[:], scalar1=1.0 / D),
                 waits=[("rch", 1)], incs=[("rch", 1)])
            P.do("vector", lambda v: v.tensor_mul(out=vrow[:], in0=sumrow[:], in1=sumrow[:]),
                 waits=[("xl", xl0 + 16 * NTB), ("rch", 2)], incs=[("rch", 1)])
            P.do("vector", lambda v: v.tensor_sub(out=sqrow[:], in0=sqrow[:], in1=vrow[:]),
                 waits=[("rch", 3)], incs=[("rch", 1)])
            P.do("scalar", lambda a: a.activation(out=vrow[:], in_=sqrow[:], func=AF.Sqrt, bias=epsB[0:1, 0:1]),
                 waits=[("rch", 4), ("cst", 1)], incs=[("rch", 1)])
            P.do("vector", lambda v: v.reciprocal(out=sqrow[:], in_=vrow[:]),
                 waits=[("rch", 5)], incs=[("rch", 1)])
            P.do("vector", lambda v: v.scalar_tensor_tensor(
                 out=sumrow[:], in0=sumrow[:], scalar=-1.0, in1=sqrow[:], op0=ALU.mult, op1=ALU.mult),
                 waits=[("rch", 6)], incs=[("rch", 1)])


        P.paranoid = "proj" in paranoid
        if "proj" in phases:
            xl0 = P.n("xl")
            for tb in range(NTB):
                pb = tb % 2
                b = tb // 4
                x_load(pb, tb, [("pt1", max(0, 8 * (tb - 1))), ("rch", 7 if "stats" in phases else 0)])
                P.do("tensor", lambda t, tb=tb: t.matmul(
                     bank(3), ones1[:], sumrow[0:1, tb * TBW:(tb + 1) * TBW], start=True, stop=True),
                     waits=[("rch", 7 if "stats" in phases else 0), ("pbe", max(0, 2 * tb - 1))],
                     incs=[("pbc", 1)])
                P.do("tensor", lambda t, tb=tb: t.matmul(
                     bank(4), ones1[:], sqrow[0:1, tb * TBW:(tb + 1) * TBW], start=True, stop=True),
                     waits=[("pbe", max(0, 2 * tb))],
                     incs=[("pbc", 1)])
                P.do("vector", lambda v: v.tensor_copy(out=c0B[:], in_=bank(3)),
                     waits=[("pbc", 2 * tb + 1)], incs=[("pbe", 1)])
                P.do("vector", lambda v: v.tensor_copy(out=c1B[:], in_=bank(4)),
                     waits=[("pbc", 2 * tb + 2)], incs=[("pbe", 1)])
                for k in range(KT):
                    tsl = k % 2
                    m = 8 * tb + k
                    P.do("vector", lambda v, pb=pb, k=k, tsl=tsl: v.tensor_mul(
                        out=tmpf[tsl][:], in0=xs[pb][:, k * TBW:(k + 1) * TBW], in1=c1B[:]),
                        waits=[("xl", xl0 + 16 * (tb + 1)), ("pbe", 2 * tb + 2),
                               ("pmx", max(0, m - 1)), ("wos", 128)],
                        incs=[("pt1", 1)])
                    P.do("vector", lambda v, tsl=tsl: v.tensor_add(
                        out=tmpf[tsl][:], in0=tmpf[tsl][:], in1=c0B[:]),
                        waits=[("pt1", 8 * tb + k + 1)], incs=[("pt2", 1)])
                    P.do("vector", lambda v, pb=pb, k=k, tsl=tsl: v.tensor_copy(
                        out=xnb[pb][:, k, :], in_=tmpf[tsl][:]),
                        waits=[("pmm", max(0, 16 * (tb - 1))), ("pt2", 8 * tb + k + 1)],
                        incs=[("pxn", 1)])
                    P.do("vector", lambda v, tsl=tsl: v.tensor_reduce(
                        out=tmpc[:], in_=tmpf[tsl][:], axis=AX.X, op=ALU.max,
                        apply_absolute_value=True),
                        waits=[("pt2", 8 * tb + k + 1)], incs=[("pmr", 1)])
                    P.do("vector", lambda v: v.tensor_max(out=maxacc[:], in0=maxacc[:], in1=tmpc[:]),
                         waits=[("pmr", 8 * tb + k + 1)], incs=[("pmx", 1)])
                for w in range(2):
                    for f in range(FT):
                        g = tb * 16 + w * 8 + f
                        m = tb * 8 + f
                        bk = g % 3
                        for k in range(KT):
                            wv = Wqb if w == 0 else Wkvb
                            P.do("tensor", lambda t, pb=pb, f=f, k=k, bk=bk, wv=wv: t.matmul(
                                bank(bk), wv[k][:, f * 128:(f + 1) * 128], xnb[pb][:, k, :],
                                start=(k == 0), stop=(k == KT - 1)),
                                waits=[("pxn", 8 * tb + k + 1), ("wqc", 16),
                                       ("pev", max(0, g - 2))],
                                incs=([("pmm", 1)] if k == KT - 1 else []))
                        ssl = m % 2
                        stg = qst if w == 0 else kvt
                        if w == 0:
                            sp_prev = ((m - 2) // 8) * 16 + ((m - 2) % 8) + 1 if m >= 2 else 0
                            extra = []
                        else:
                            sp_prev = ((m - 2) // 8) * 16 + 8 + ((m - 2) % 8) + 1 if m >= 2 else 0
                            extra = [("ptr", max(0, m - 1))]
                        P.do("scalar", lambda a, bk=bk, ssl=ssl, stg=stg, w=w: a.activation(
                            out=stg[ssl][:], in_=bank(bk), func=AF.Copy,
                            scale=(0.125 if w == 0 else 1.0)),
                            waits=[("pmm", g + 1), ("psp", 16 * sp_prev)] + extra,
                            incs=[("pev", 1)])
                        dst = qs if w == 0 else kvs
                        P.do("sync", lambda s, f=f, tb=tb, ssl=ssl, stg=stg, dst=dst: s.dma_start(
                            out=dst[f][:, tb * TBW:(tb + 1) * TBW], in_=stg[ssl][:]),
                            waits=[("pev", g + 1)], incs=[("psp", 16)])
                        if w == 1:
                            ph = m % 2
                            for i in range(4):
                                P.do("tensor", lambda t, ssl=ssl, i=i, ph=ph: t.transpose(
                                    PST[:, ph * 1024 + i * 128: ph * 1024 + (i + 1) * 128],
                                    kvt[ssl][:, i * 128:(i + 1) * 128], idB[:]),
                                    waits=[("pev", g + 1), ("pte", max(0, m - 1))])
                            P.do("tensor", lambda t: t.matmul(
                                PSB[0:1, 0:1], ones1[0:1, 0:1], ones1[0:1, 0:1],
                                start=True, stop=True),
                                incs=[("ptr", 1)])
                            P.do("vector", lambda v, ph=ph, ksl=m % 2: v.tensor_copy(
                                out=ktt[ksl][:], in_=PST[:, ph * 1024: ph * 1024 + 512]),
                                waits=[("ptr", m + 1), ("pts", max(0, 16 * (m - 1)))],
                                incs=[("pte", 1)])
                            P.do("sync", lambda s, b=b, f=f, tb=tb, ksl=m % 2: s.dma_start(
                                out=kts[b, f, (tb % 4) * 4:(tb % 4) * 4 + 4].rearrange("c p d -> p c d"),
                                in_=ktt[ksl].rearrange("p (c d) -> p c d", d=128)),
                                waits=[("pte", m + 1)], incs=[("pts", 16)])
            if "dmaxa" in dbg_out:
                P.do("sync", lambda s: s.dma_start(out=dmaxa[:], in_=maxacc[:]),
                     waits=[("pmx", 64)], incs=[("dmz", 16)])
            P.do("vector", lambda v: v.tensor_copy(out=ccs[:, 0:1], in_=maxacc[:]),
                 waits=[("pmx", 64), ("cst", 2)], incs=[("gmx", 1)])
            P.do("sync", lambda s: s.dma_start(out=ccin1[:], in_=ccs[:]),
                 waits=[("gmx", 1)], incs=[("gst", 16)])
            P.do("gpsimd", lambda g: g.collective_compute(
                "AllReduce", ALU.max, replica_groups=[list(range(8))],
                ins=[ccin1[:]], outs=[ccout1[:]]),
                waits=[("gst", 16)], incs=[("cc1", None)])
            P.do("sync", lambda s: s.dma_start(out=ccr[:], in_=ccout1[:]),
                 waits=[("cc1", 1)], incs=[("gld", 16)])
            if "dbgcc" in dbg_out:
                P.do("sync", lambda s: s.dma_start(out=dbgcc[:, 0:16], in_=ccs[:]),
                     waits=[("gmx", 1)], incs=[("dcc", 16)])
                P.do("sync", lambda s: s.dma_start(out=dbgcc[:, 16:32], in_=ccr[:]),
                     waits=[("gld", 16)], incs=[("dcc", 16)])
            pmax_chain("a1", ccr[:, 0:1], 0, rB, sxB,
                       [("gld", 16), ("pev", 128)])
            P.do("vector", lambda v: v.tensor_copy(out=mrow[0:1, 12:13], in_=rB[0:1, :]),
                 waits=[("a1ch", 5)], incs=[("a1d", 1)])
            P.do("vector", lambda v: v.tensor_copy(out=mrow[0:1, 13:14], in_=sxB[0:1, :]),
                 incs=[("a1d", 1)])
            P.do("vector", lambda v: v.tensor_copy(out=mrow[0:1, 14:15], in_=mtmp[0:1, :]),
                 incs=[("a1d", 1)])
            P.do("vector", lambda v: v.tensor_copy(out=mrow[0:1, 15:16], in_=rB[64:65, :]),
                 incs=[("a1d", 1)])

        P.paranoid = "xq" in paranoid
        if "xq" in phases:
            nproj_mm = P.n("pmm")
            xl0 = P.n("xl")
            P.do("vector", lambda v: v.tensor_scalar_mul(out=sumrow[:], in0=sumrow[:],
                                                         scalar1=rB[0:1, 0:1]),
                 waits=[("pbc", 16 if "proj" in phases else 0), ("a1ch", 5)], incs=[("qcs", 1)])
            P.do("vector", lambda v: v.tensor_scalar_mul(out=sqrow[:], in0=sqrow[:],
                                                         scalar1=rB[0:1, 0:1]),
                 waits=[("qcs", 1)], incs=[("qcs", 1)])
            for tb in range(NTB):
                pb = tb % 2
                x_load(pb, tb, [("xqd", max(0, 8 * (tb - 1))), ("pt1", 64 if "proj" in phases else 0)])
                P.do("tensor", lambda t, tb=tb: t.matmul(
                     bank(3), ones1[:], sumrow[0:1, tb * TBW:(tb + 1) * TBW], start=True, stop=True),
                     waits=[("qcs", 2), ("qbe", max(0, 2 * tb - 1)),
                            ("pbe", 16 if "proj" in phases else 0),
                            ("pt1", 64 if "proj" in phases else 0)],
                     incs=[("qbc", 1)])
                P.do("tensor", lambda t, tb=tb: t.matmul(
                     bank(4), ones1[:], sqrow[0:1, tb * TBW:(tb + 1) * TBW], start=True, stop=True),
                     waits=[("qbe", max(0, 2 * tb))],
                     incs=[("qbc", 1)])
                P.do("vector", lambda v: v.tensor_copy(out=c0B[:], in_=bank(3)),
                     waits=[("qbc", 2 * tb + 1)], incs=[("qbe", 1)])
                P.do("vector", lambda v: v.tensor_copy(out=c1B[:], in_=bank(4)),
                     waits=[("qbc", 2 * tb + 2)], incs=[("qbe", 1)])
                for k in range(KT):
                    tsl = k % 2
                    P.do("vector", lambda v, pb=pb, k=k, tsl=tsl: v.tensor_mul(
                        out=tmpf[tsl][:], in0=xs[pb][:, k * TBW:(k + 1) * TBW], in1=c1B[:]),
                        waits=[("xl", xl0 + 16 * (tb + 1)),
                               ("qbe", 2 * tb + 2), ("xqd", max(0, 8 * tb + k - 1))],
                        incs=[("qt1", 1)])
                    P.do("vector", lambda v, tsl=tsl: v.tensor_add(
                        out=tmpf[tsl][:], in0=tmpf[tsl][:], in1=c0B[:]),
                        waits=[("qt1", 8 * tb + k + 1)], incs=[("qt2", 1)])
                    P.do("vector", lambda v, tb=tb, k=k, tsl=tsl: v.tensor_scalar(
                        out=xqv[k][:, tb * TBW:(tb + 1) * TBW], in0=tmpf[tsl][:],
                        scalar1=C_ROUND, scalar2=-C_ROUND, op0=ALU.add, op1=ALU.add),
                        waits=[("pmm", 128), ("qt2", 8 * tb + k + 1)],
                        incs=[("xqd", 1)])

        P.paranoid = "attn" in paranoid
        if "attn" in phases:
            for b in range(2):
                for hp in range(HP):
                    g = b * HP + hp
                    sl = g % 2
                    lw = [("amst", max(0, 64 * (g - 1))), ("avm", max(0, 64 * (g - 1))),
                          ("psp", 16 * 128 if "proj" in phases else 0),
                          ("asp", 512 * b)]
                    P.do("sync", lambda s, b=b, hp=hp, sl=sl: s.dma_start(
                        out=qhp[sl][:], in_=qs[hp][:, b * SEQ:(b + 1) * SEQ]),
                        waits=lw, incs=[("al", 16)])
                    P.do("sync", lambda s, b=b, hp=hp, sl=sl: s.dma_start(
                        out=khp[sl][:], in_=kvs[hp][:, b * SEQ:(b + 1) * SEQ]),
                        incs=[("al", 16)])
                    P.do("sync", lambda s, b=b, hp=hp, sl=sl: s.dma_start(
                        out=tokhp[sl].rearrange("p c d -> p c d"),
                        in_=kts[b, hp].rearrange("c p d -> p c d"),),
                        waits=[("pts", 16 * 64 if "proj" in phases else 0)],
                        incs=[("al", 16)])
                    for qq in range(QQ):
                        uq = g * QQ + qq
                        avp = uq % 2
                        for tc in range(TC):
                            u = uq * TC + tc
                            stp = tc % 2
                            P.do("tensor", lambda t, sl=sl, qq=qq, tc=tc, stp=stp: t.matmul(
                                bank(2 * stp), khp[sl][0:64, tc * 128:(tc + 1) * 128],
                                qhp[sl][0:64, qq * TBW:(qq + 1) * TBW],
                                start=True, stop=True, tile_position=(0, 0)),
                                waits=[("al", 48 * (g + 1)), ("aev", max(0, u - 1)),
                                       ("qbe", 16 if (u == 0 and "xq" in phases) else 0),
                                       ("a1ch", 1 if (u == 0 and "proj" in phases) else 0),
                                       ("ape", 32 * b if (hp == 0 and qq == 0 and tc == 0) else 0)])
                            P.do("tensor", lambda t, sl=sl, qq=qq, tc=tc, stp=stp: t.matmul(
                                bank(2 * stp + 1), khp[sl][64:128, tc * 128:(tc + 1) * 128],
                                qhp[sl][64:128, qq * TBW:(qq + 1) * TBW],
                                start=True, stop=True, tile_position=(64, 0)),
                                incs=[("amst", 1)])
                            if tc % 2 == 0:
                                P.do("scalar", lambda a, stp=stp, bsl=tc % 2: a.activation(
                                    out=stbuf[bsl][:], in_=PSA[:, stp * 1024: stp * 1024 + 1024],
                                    func=AF.Relu),
                                    waits=[("amst", u + 1), ("avm", max(0, u - 1))],
                                    incs=[("aev", 1)])
                            else:
                                P.do("vector", lambda v, stp=stp, bsl=tc % 2: v.tensor_scalar_max(
                                    out=stbuf[bsl][:], in0=PSA[:, stp * 1024: stp * 1024 + 1024],
                                    scalar1=0.0),
                                    waits=[("amst", u + 1), ("avm", max(0, u - 1))],
                                    incs=[("aev", 1)])
                            avt = PSA[:, 2048:2560] if avp == 0 else PSB[:, 0:512]
                            P.do("tensor", lambda t, sl=sl, tc=tc, bsl=tc % 2, avt=avt: t.matmul(
                                avt[0:64, :],
                                tokhp[sl][:, tc, 0:64], stbuf[bsl][:, 0:TBW],
                                start=(tc == 0), stop=(tc == TC - 1), tile_position=(0, 0)),
                                waits=[("aev", u + 1), ("aoe", max(0, uq - 1))])
                            P.do("tensor", lambda t, sl=sl, tc=tc, bsl=tc % 2, avt=avt: t.matmul(
                                avt[64:128, :],
                                tokhp[sl][:, tc, 64:128], stbuf[bsl][:, TBW:2 * TBW],
                                start=(tc == 0), stop=(tc == TC - 1), tile_position=(0, 64)),
                                incs=[("avm", 1)])
                        eng = "scalar" if uq % 2 == 0 else "vector"
                        avt = PSA[:, 2048:2560] if avp == 0 else PSB[:, 0:512]
                        emit = (lambda e, hp=hp, qq=qq, avt=avt, eng=eng:
                                (e.copy if eng == "scalar" else e.tensor_copy)(
                                    out=outTT[hp][:, qq * TBW:(qq + 1) * TBW], in_=avt))
                        P.do(eng, emit,
                             waits=[("avm", 16 * (uq + 1)), ("apm", 32 * b)],
                             incs=[("aoe", 1)])
                for dt in range(8):
                    gd = b * 8 + dt
                    wsl = gd % 2
                    P.do("sync", lambda s, dt=dt, wsl=wsl: s.dma_start(
                        out=wst[wsl].rearrange("p f d -> p f d"),
                        in_=wouts[dt].rearrange("f p d -> p f d")),
                        waits=[("wos", 128), ("apm", max(0, 4 * (gd - 1)))],
                        incs=[("awl", 16)])
                    for qq in range(QQ):
                        gq = gd * QQ + qq
                        obk = gq % 2
                        for f in range(FT):
                            P.do("tensor", lambda t, f=f, qq=qq, wsl=wsl, obk=obk: t.matmul(
                                bank(obk), wst[wsl][:, f, :], outTT[f][:, qq * TBW:(qq + 1) * TBW],
                                start=(f == 0), stop=(f == FT - 1)),
                                waits=[("awl", 16 * (gd + 1)), ("aoe", 32 * (b + 1)),
                                       ("ape", max(0, gq - 1))],
                                incs=([("apm", 1)] if f == FT - 1 else []))
                        asl = gq % 2
                        P.do("scalar", lambda a, dt=dt, asl=asl, obk=obk: a.activation(
                            out=aos[asl][:], in_=bank(obk), func=AF.Identity, bias=boutT[:, dt:dt + 1]),
                            waits=[("apm", gq + 1), ("asp", max(0, 16 * (gq - 1)))],
                            incs=[("ape", 1)])
                        P.do("sync", lambda s, b=b, dt=dt, qq=qq, asl=asl: s.dma_start(
                            out=att_s[b, dt][:, qq * TBW:(qq + 1) * TBW], in_=aos[asl][:]),
                            waits=[("ape", gq + 1)], incs=[("asp", 16)])

        P.paranoid = "ffn1" in paranoid
        if "ffn1" in phases:
            post_attn = [("asp", 1024 if "attn" in phases else 0)]
            for k in range(KT):
                sl = k % 2
                P.do("sync", lambda s, k=k, sl=sl: s.dma_start(
                    out=wls[sl][:], in_=w1T[k * 128:(k + 1) * 128, :]),
                    waits=[("f1wr", max(0, k - 1))] + post_attn,
                    incs=[("f1wl", 16)])
                P.do("vector", lambda v, sl=sl: v.tensor_reduce(
                    out=tmpc[:], in_=wls[sl][:], axis=AX.X, op=ALU.max, apply_absolute_value=True),
                    waits=[("f1wl", 16 * (k + 1)), ("f1wr", max(0, k))], incs=[("f1wr0", 1)])
                P.do("vector", lambda v: v.tensor_max(out=w1acc[:], in0=w1acc[:], in1=tmpc[:]),
                     waits=[("f1wr0", k + 1)], incs=[("f1wr", 1)])
            pmax_chain("w1", w1acc[:], 4, rw1B, sw1B,
                       [("f1wr", KT), ("apm", 64 if "attn" in phases else 0),
                        ("aoe", 64 if "attn" in phases else 0)])
            P.do("vector", lambda v: v.tensor_mul(out=s1B[:], in0=sw1B[:], in1=sxB[:]),
                 waits=[("w1ch", 5), ("a1ch", 5 if "proj" in phases else 0)], incs=[("f1sc", 1)])
            P.do("vector", lambda v: v.tensor_copy(out=mrow[0:1, 5:6], in_=s1B[0:1, :]), incs=[("dsc", 1)])
            P.do("vector", lambda v: v.tensor_copy(out=mrow[0:1, 6:7], in_=rB[0:1, :]), incs=[("dsc", 1)])
            P.do("vector", lambda v: v.tensor_copy(out=mrow[0:1, 7:8], in_=sxB[0:1, :]), incs=[("dsc", 1)])
            P.do("vector", lambda v: v.tensor_copy(out=mrow[0:1, 3:4], in_=rw1B[0:1, :]), incs=[("dsc", 1)])
            P.do("vector", lambda v: v.tensor_scalar_mul(out=b1s[:], in0=b1T[:], scalar1=s1B[:]),
                 waits=[("f1sc", 1), ("bte", 48)], incs=[("f1be", 3)])
            for k in range(KT):
                sl = k % 2
                P.do("sync", lambda s, k=k, sl=sl: s.dma_start(
                    out=wls[sl][:], in_=w1T[k * 128:(k + 1) * 128, :]),
                    waits=[("f1qz", max(0, k - 1)), ("f1be", 3),
                           ("qbc", 16 if "xq" in phases else 0)],
                    incs=[("f1w2", 16)])
                P.do("scalar", lambda a, sl=sl: a.activation(
                    out=wls[sl][:], in_=wls[sl][:], func=AF.Copy, scale=rw1B[:], bias=C_ROUND),
                    waits=[("f1w2", 16 * (k + 1))], incs=[("f1qa", 1)])
                P.do("vector", lambda v, k=k, sl=sl: v.tensor_scalar_add(
                    out=w1q[k][:], in0=wls[sl][:], scalar1=-C_ROUND),
                    waits=[("f1qa", k + 1)], incs=[("f1qz", 1)])
            for tb in range(NTB):
                for jt in range(JT):
                    seq = tb * JT + jt
                    bk = seq % 5
                    for k in range(KT):
                        P.do("tensor", lambda t, tb=tb, jt=jt, k=k, bk=bk: t.matmul(
                            bank(bk), w1q[k][:, jt * 128:(jt + 1) * 128],
                            xqv[k][:, tb * TBW:(tb + 1) * TBW],
                            start=(k == 0), stop=(k == KT - 1)),
                            waits=[("f1qz", KT), ("xqd", 64),
                                   ("f1e", max(0, seq - 4))],
                            incs=([("f1m", 1)] if k == KT - 1 else []))
                    ssl = seq % 4
                    P.do("scalar", lambda a, bk=bk, ssl=ssl, jt=jt: a.activation(
                        out=hst[ssl][:], in_=bank(bk), func=AF.Gelu,
                        scale=s1B[:], bias=b1s[:, jt:jt + 1]),
                        waits=[("f1m", seq + 1), ("f1s", max(0, 16 * (seq - 3))),
                               ("f1r", max(0, 2 * (seq - 3)))],
                        incs=[("f1e", 1)])
                    P.do("sync", lambda s, jt=jt, tb=tb, ssl=ssl: s.dma_start(
                        out=hsp[jt][:, tb * TBW:(tb + 1) * TBW], in_=hst[ssl][:]),
                        waits=[("f1e", seq + 1)], incs=[("f1s", 16)])
                    P.do("vector", lambda v, ssl=ssl: v.tensor_reduce(
                        out=tmpc[:], in_=hst[ssl][:], axis=AX.X, op=ALU.max,
                        apply_absolute_value=True),
                        waits=[("f1e", seq + 1), ("f1r", max(0, 2 * seq))], incs=[("f1r0", 1)])
                    P.do("vector", lambda v: v.tensor_max(out=hacc[:], in0=hacc[:], in1=tmpc[:]),
                         waits=[("f1r0", seq + 1)], incs=[("f1r", 2)])
            P.do("vector", lambda v: v.tensor_copy(out=ccs[:, 0:1], in_=hacc[:]),
                 waits=[("f1r", 2 * NTB * JT)], incs=[("ghx", 1)])
            P.do("sync", lambda s: s.dma_start(out=ccin2[:], in_=ccs[:]),
                 waits=[("ghx", 1)], incs=[("gst2", 16)])
            P.do("gpsimd", lambda g: g.collective_compute(
                "AllReduce", ALU.max, replica_groups=[list(range(8))],
                ins=[ccin2[:]], outs=[ccout2[:]]),
                waits=[("gst2", 16)], incs=[("cc2", None)])
            P.do("sync", lambda s: s.dma_start(out=ccr[:], in_=ccout2[:]),
                 waits=[("cc2", 1)], incs=[("gld2", 16)])
            pmax_chain("hh", ccr[:, 0:1], 8, rhB, shB,
                       [("gld2", 16), ("f1e", 256)])
            if "dxq" in dbg_out:
                for k in range(KT):
                    P.do("sync", lambda s, k=k: s.dma_start(out=dxq[k], in_=xqv[k][:]),
                         waits=[("xqd", 64), ("f1qz", KT)], incs=[("ddmp", 16)])
                    P.do("sync", lambda s, k=k: s.dma_start(out=dw1q[k], in_=w1q[k][:]),
                         incs=[("ddmp", 16)])
                P.do("gpsimd", lambda g: g.dma_start(out=mxs[:], in_=mrow[:]),
                     waits=[("gld2", 16), ("dsc", 4)], incs=[("ddmp", 16)])
                P.do("sync", lambda s: None, waits=[("ddmp", P.n("ddmp"))])

        P.paranoid = "ffn2" in paranoid
        if "ffn2" in phases:
            for j in range(JT):
                sl = j % 2
                P.do("sync", lambda s, j=j, sl=sl: s.dma_start(
                    out=w2ls[sl][:], in_=w2T[j * 128:(j + 1) * 128, :]),
                    waits=[("f2wr", max(0, j - 1)),
                           ("apm", 64 if "attn" in phases else 0)],
                    incs=[("f2wl", 16)])
                P.do("vector", lambda v, sl=sl: v.tensor_reduce(
                    out=tmpc[:], in_=w2ls[sl][:], axis=AX.X, op=ALU.max, apply_absolute_value=True),
                    waits=[("f2wl", 16 * (j + 1)), ("f2wr", max(0, j))], incs=[("f2wr0", 1)])
                P.do("vector", lambda v: v.tensor_max(out=w2acc[:], in0=w2acc[:], in1=tmpc[:]),
                     waits=[("f2wr0", j + 1)], incs=[("f2wr", 1)])
            pmax_chain("w2", w2acc[:], 12, rw2B, sw2B,
                       [("f2wr", JT), ("f1e", 256 if "ffn1" in phases else 0),
                        ("hhch", 5 if "ffn1" in phases else 0)])
            P.do("vector", lambda v: v.tensor_mul(out=s2B[:], in0=sw2B[:], in1=shB[:]),
                 waits=[("w2ch", 5), ("hhch", 5 if "ffn1" in phases else 0)], incs=[("f2sc", 1)])
            P.do("vector", lambda v: v.tensor_scalar_mul(out=b2s[:], in0=b2T[:], scalar1=s2B[:]),
                 waits=[("f2sc", 1), ("bte", 48)], incs=[("f2be", 4)])
            for j in range(JT):
                sl = j % 2
                P.do("sync", lambda s, j=j, sl=sl: s.dma_start(
                    out=w2ls[sl][:], in_=w2T[j * 128:(j + 1) * 128, :]),
                    waits=[("f2qz", max(0, j - 1)), ("f2be", 4)],
                    incs=[("f2w2", 16)])
                P.do("scalar", lambda a, sl=sl: a.activation(
                    out=w2ls[sl][:], in_=w2ls[sl][:], func=AF.Copy, scale=rw2B[:], bias=C_ROUND),
                    waits=[("f2w2", 16 * (j + 1))], incs=[("f2qa", 1)])
                P.do("vector", lambda v, j=j, sl=sl: v.tensor_scalar_add(
                    out=w2q[j][:], in0=w2ls[sl][:], scalar1=-C_ROUND),
                    waits=[("f2qa", j + 1)], incs=[("f2qz", 1)])
            hspr = hsp.rearrange("(g c) p w -> g c p w", c=4)
            for tb in range(NTB):
                b = tb // 4
                qcol = (tb % 4) * TBW
                for jg in range(8):
                    u = tb * 8 + jg
                    sl = u % 2
                    P.do("sync", lambda s, tb=tb, jg=jg, sl=sl: s.dma_start(
                        out=hld[sl].rearrange("p (c w) -> p c w", w=TBW),
                        in_=hspr[jg][:, :, tb * TBW:(tb + 1) * TBW].rearrange("c p w -> p c w")),
                        waits=[("hqd", max(0, u - 1)),
                               ("f1s", 16 * NTB * JT if "ffn1" in phases else 0)],
                        incs=[("hxl", 16)])
                    P.do("scalar", lambda a, sl=sl: a.activation(
                        out=hld[sl][:], in_=hld[sl][:], func=AF.Copy, scale=rhB[:], bias=C_ROUND),
                        waits=[("hxl", 16 * (u + 1)), ("f2be", 4)], incs=[("hqa", 1)])
                    P.do("vector", lambda v, jg=jg, sl=sl: v.tensor_scalar_add(
                        out=hqv[:, jg * 2048:(jg + 1) * 2048], in0=hld[sl][:], scalar1=-C_ROUND),
                        waits=[("hqa", u + 1), ("f2m", max(0, 8 * (tb - 1)))],
                        incs=[("hqd", 1)])
                P.do("sync", lambda s, b=b, qcol=qcol: s.dma_start(
                    out=atts[:],
                    in_=att_s[b][:, :, qcol:qcol + TBW].rearrange("dt p w -> p dt w")),
                    waits=[("asp", 1024 if "attn" in phases else 0),
                           ("f2qz", JT),
                           ("yev", max(0, 16 * (tb - 1)))],
                    incs=[("f2al", 16)])
                for dt in range(8):
                    seq = tb * 8 + dt
                    bk = seq % 4
                    xsl_sl = seq % 2
                    P.do("sync", lambda s, dt=dt, tb=tb, xsl_sl=xsl_sl: s.dma_start(
                        out=xsl[xsl_sl][:], in_=xT[dt * 128:(dt + 1) * 128, tb * TBW:(tb + 1) * TBW]),
                        waits=[("yev", max(0, 2 * seq - 2))],
                        incs=[("f2xl", 16)])
                    for jt in range(JT):
                        P.do("tensor", lambda t, jt=jt, dt=dt, bk=bk: t.matmul(
                            bank(bk), w2q[jt][:, dt * 128:(dt + 1) * 128],
                            hqv[:, jt * TBW:(jt + 1) * TBW],
                            start=(jt == 0), stop=(jt == JT - 1)),
                            waits=[("hqd", 8 * (tb + 1)), ("f2qz", JT),
                                   ("f2e", max(0, seq - 3))],
                            incs=([("f2m", 1)] if jt == JT - 1 else []))
                    ysl = seq % 2
                    P.do("scalar", lambda a, bk=bk, dt=dt, ysl=ysl: a.activation(
                        out=yst[ysl][:], in_=bank(bk), func=AF.Identity, scale=s2B[:],
                        bias=b2s[:, dt:dt + 1]),
                        waits=[("f2m", seq + 1), ("yw", max(0, 16 * (seq - 1)))],
                        incs=[("f2e", 1)])
                    P.do("vector", lambda v, dt=dt, ysl=ysl: v.tensor_add(
                        out=yst[ysl][:], in0=yst[ysl][:], in1=atts[:, dt, :]),
                        waits=[("f2e", seq + 1), ("f2al", 16 * (tb + 1))],
                        incs=[("yev", 1)])
                    P.do("vector", lambda v, ysl=ysl, xsl_sl=xsl_sl: v.tensor_add(
                        out=yst[ysl][:], in0=yst[ysl][:], in1=xsl[xsl_sl][:]),
                        waits=[("f2xl", 16 * (seq + 1)), ("yev", 2 * seq + 1)],
                        incs=[("yev", 1)])
                    P.do("sync", lambda s, dt=dt, tb=tb, ysl=ysl: s.dma_start(
                        out=yT[dt * 128:(dt + 1) * 128, tb * TBW:(tb + 1) * TBW], in_=yst[ysl][:]),
                        waits=[("yev", 2 * (seq + 1))], incs=[("yw", 16)])
            P.do("sync", lambda s: None, waits=[("yw", 16 * 64)])
        else:
            for snm in ("psp", "pts", "asp", "f1s", "wos"):
                if P.n(snm):
                    P.do("sync", lambda e, snm=snm: None, waits=[(snm, P.n(snm))])
            if "proj" in phases:
                P.do("gpsimd", lambda g: g.dma_start(out=mxs[:], in_=mrow[:]),
                     waits=[("a1ch", 5), ("a1d", 4)], incs=[("dmx", 16)])
                P.do("sync", lambda s: None, waits=[("dmx", 16)])
            if "xq" in phases and "dxq" in dbg_out:
                for k in range(KT):
                    P.do("sync", lambda s, k=k: s.dma_start(out=dxq[k], in_=xqv[k][:]),
                         waits=[("xqd", 64)], incs=[("ddmp", 16)])
                P.do("sync", lambda s: None, waits=[("ddmp", P.n("ddmp"))])

        with nc.Block() as block:
            P.replay(block)

    return nc


def make_inputs(x, Wq, Wkv, Wout, bout, w1, b1, w2, b2):
    import ml_dtypes
    w1T = np.ascontiguousarray(w1.T)
    w2T = np.ascontiguousarray(w2.T)
    b1m = np.ascontiguousarray(b1.reshape(32, 128).T)
    b2m = np.ascontiguousarray(b2.reshape(8, 128).T)
    boutm = np.ascontiguousarray(bout.reshape(8, 128).T)
    identF = np.eye(128, dtype=np.float32)
    identB = np.eye(128, dtype=ml_dtypes.bfloat16)
    in_maps = []
    for c in range(8):
        xTc = np.ascontiguousarray(x[2 * c:2 * c + 2].transpose(2, 0, 1).reshape(D, T))
        in_maps.append({
            "xT": xTc, "Wq": Wq, "Wkv": Wkv, "Wout": Wout,
            "w1T": w1T, "w2T": w2T, "b1m": b1m, "b2m": b2m, "boutm": boutm,
            "identF": identF, "identB": identB,
        })
    return in_maps


def kernel(L0_attn_hidden_states, L0_attn_Wq, L0_attn_Wkv, L0_attn_Wout, L0_attn_bout,
           L0_ffn_w1, L0_ffn_b1, L0_ffn_w2, L0_ffn_b2):
    x = np.asarray(L0_attn_hidden_states, dtype=np.float32)
    Wq = np.ascontiguousarray(np.asarray(L0_attn_Wq, dtype=np.float32))
    Wkv = np.ascontiguousarray(np.asarray(L0_attn_Wkv, dtype=np.float32))
    Wout = np.ascontiguousarray(np.asarray(L0_attn_Wout, dtype=np.float32))
    bout = np.asarray(L0_attn_bout, dtype=np.float32)
    w1 = np.asarray(L0_ffn_w1, dtype=np.float32)
    b1 = np.asarray(L0_ffn_b1, dtype=np.float32)
    w2 = np.asarray(L0_ffn_w2, dtype=np.float32)
    b2 = np.asarray(L0_ffn_b2, dtype=np.float32)

    nc = build_nc()
    in_maps = make_inputs(x, Wq, Wkv, Wout, bout, w1, b1, w2, b2)
    res = run_bass_kernel_spmd(nc, in_maps, core_ids=list(range(8)))
    y = np.empty((16, 2048, 1024), dtype=np.float32)
    for c in range(8):
        yTc = np.asarray(res.results[c]["yT"])
        y[2 * c:2 * c + 2] = yTc.reshape(D, 2, SEQ).transpose(1, 2, 0)
    return y


# version 03
# speedup vs baseline: 1.5948x; 1.4064x over previous
import numpy as np
from contextlib import ExitStack

import concourse.bass as bass
import concourse.mybir as mybir
from concourse.bass_utils import run_bass_kernel_spmd

F32 = mybir.dt.float32
BF16 = mybir.dt.bfloat16
AF = mybir.ActivationFunctionType
ALU = mybir.AluOpType
AX = mybir.AxisListType

D = 1024
T = 4096
SEQ = 2048
J = 4096
TBW = 512
NTB = 8
KT = 8
FT = 8
JT = 32
HP = 8
QQ = 4
TC = 16
C_ROUND = float(np.float32(3 << 22))
QP = 127.0
MIN_RANGE = 1e-6
LN_EPS = 1e-5

KB = 1024
S0, S1, S2, S3 = 0, 16 * KB, 32 * KB, 48 * KB
XQR = 64 * KB
WQR = 128 * KB
ARENA = 192 * KB


class Prog:
    def __init__(self, nc, ctx, paranoid=False):
        self.nc = nc
        self.ctx = ctx
        self.ops = []
        self.sems = {}
        self.counts = {}
        self.paranoid = paranoid

    def sem(self, name):
        if name not in self.sems:
            self.sems[name] = self.ctx.enter_context(self.nc.semaphore(name))
            self.counts[name] = 0
        return self.sems[name]

    def n(self, name):
        return self.counts.get(name, 0)

    def do(self, engine, emit, waits=(), incs=()):
        if self.paranoid:
            waits = list(waits) + [(s, v) for s, v in self.counts.items()]
        waits = [(s, v) for (s, v) in waits if v > 0]
        wd = {}
        for s, v in waits:
            wd[s] = max(wd.get(s, 0), v)
        waits = list(wd.items())
        for s, _ in waits:
            self.sem(s)
        assert len(incs) <= 1
        for s, k in incs:
            self.sem(s)
            self.counts[s] += (k if k is not None else 1)
        self.ops.append((engine, emit, waits, list(incs)))

    def replay(self, block):
        def mk(eng_name):
            def body(eng):
                hi = {}
                for engine, emit, waits, incs in self.ops:
                    if engine != eng_name:
                        continue
                    for s, v in waits:
                        if v > hi.get(s, 0):
                            eng.wait_ge(self.sems[s], v)
                            hi[s] = v
                    ins = emit(eng)
                    for s, k in incs:
                        assert ins is not None
                        if k is None:
                            ins.then_inc(self.sems[s])
                        else:
                            ins.then_inc(self.sems[s], k)
            return body
        block.sync(mk("sync"))
        block.tensor(mk("tensor"))
        block.scalar(mk("scalar"))
        block.vector(mk("vector"))
        block.gpsimd(mk("gpsimd"))


def build_nc(phases=("stats", "proj", "xq", "attn", "ffn1", "ffn2"), dbg_out=(), paranoid=()):
    nc = bass.Bass("TRN2", target_bir_lowering=False)

    xT = nc.dram_tensor("xT", [D, T], F32, kind="ExternalInput")
    Wq = nc.dram_tensor("Wq", [D, D], F32, kind="ExternalInput")
    Wkv = nc.dram_tensor("Wkv", [D, D], F32, kind="ExternalInput")
    Wout = nc.dram_tensor("Wout", [D, D], F32, kind="ExternalInput")
    w1T = nc.dram_tensor("w1T", [D, J], F32, kind="ExternalInput")
    w2T = nc.dram_tensor("w2T", [J, D], F32, kind="ExternalInput")
    b1m = nc.dram_tensor("b1m", [128, 32], F32, kind="ExternalInput")
    b2m = nc.dram_tensor("b2m", [128, 8], F32, kind="ExternalInput")
    boutm = nc.dram_tensor("boutm", [128, 8], F32, kind="ExternalInput")
    identF = nc.dram_tensor("identF", [128, 128], F32, kind="ExternalInput")
    identB = nc.dram_tensor("identB", [128, 128], BF16, kind="ExternalInput")
    yT = nc.dram_tensor("yT", [D, T], F32, kind="ExternalOutput")

    def scratch(name, shape, dtype):
        kind = "ExternalOutput" if name in dbg_out else "Internal"
        return nc.dram_tensor(name, shape, dtype, kind=kind)

    qs = scratch("qs", [FT, 128, T], BF16)
    kvs = scratch("kvs", [FT, 128, T], BF16)
    kts = scratch("kts", [2, HP, TC, 128, 128], BF16)
    wouts = scratch("wouts", [8, 8, 128, 128], BF16)
    hsp = scratch("hsp", [JT, 128, T], F32)
    att_s = scratch("att_s", [2, 8, 128, SEQ], BF16)
    mxs = scratch("mxs", [1, 16], F32)
    dxq = scratch("dxq", [KT, 128, T], BF16)
    dmaxa = scratch("dmaxa", [128, 1], F32)
    dw1q = scratch("dw1q", [KT, 128, J], BF16)
    ccin1 = nc.dram_tensor("ccin1", [128, 16], F32)
    ccout1 = nc.dram_tensor("ccout1", [128, 16], F32)
    dbgcc = scratch("dbgcc", [128, 32], F32)
    ccin2 = nc.dram_tensor("ccin2", [128, 16], F32)
    ccout2 = nc.dram_tensor("ccout2", [128, 16], F32)

    with ExitStack() as ctx:
        arena = ctx.enter_context(nc.sbuf_tensor("arena", [128, ARENA], mybir.dt.uint8))
        base = nc.lookup_mloc(arena).addr

        def at(name, shape, dtype, off):
            return nc.alloc_sbuf_tensor_at(name, shape, dtype, offset=base + off)

        def sbt(name, shape, dtype):
            return ctx.enter_context(nc.sbuf_tensor(name, shape, dtype))

        ones1 = sbt("ones1", [1, 128], F32)
        onesC = sbt("onesC", [128, 1], F32)
        idF = sbt("idF", [128, 128], F32)
        idB = sbt("idB", [128, 128], BF16)
        b1T = sbt("b1T", [128, 32], F32)
        b2T = sbt("b2T", [128, 8], F32)
        boutT = sbt("boutT", [128, 8], F32)
        b1s = sbt("b1s", [128, 32], F32)
        b2s = sbt("b2s", [128, 8], F32)
        maxacc = sbt("maxacc", [128, 1], F32)
        w1acc = sbt("w1acc", [128, 1], F32)
        w2acc = sbt("w2acc", [128, 1], F32)
        hacc = sbt("hacc", [128, 1], F32)
        tmpc = sbt("tmpc", [128, 1], F32)
        epsB = sbt("epsB", [128, 1], F32)
        mrow = sbt("mrow", [1, 16], F32)
        s1B = sbt("s1B", [128, 1], F32)
        s2B = sbt("s2B", [128, 1], F32)
        rw1B = sbt("rw1B", [128, 1], F32)
        rw2B = sbt("rw2B", [128, 1], F32)
        rhB = sbt("rhB", [128, 1], F32)
        ccs = sbt("ccs", [128, 16], F32)
        rowb = sbt("rowb", [1, 128], F32)
        mtmp = sbt("mtmp", [128, 1], F32)
        mcl = sbt("mcl", [128, 1], F32)
        mrc = sbt("mrc", [128, 1], F32)
        rB = sbt("rB", [128, 1], F32)
        sxB = sbt("sxB", [128, 1], F32)
        shB = sbt("shB", [128, 1], F32)
        sw1B = sbt("sw1B", [128, 1], F32)
        sw2B = sbt("sw2B", [128, 1], F32)
        ccr = sbt("ccr", [128, 16], F32)
        stbuf = [sbt(f"stbuf{i}", [128, 1024], BF16) for i in range(4)]
        aos = [sbt(f"aos{i}", [128, TBW], BF16) for i in range(2)]
        ktt = [sbt(f"ktt{i}", [128, TBW], BF16) for i in range(2)]

        PSA = ctx.enter_context(nc.psum_tensor("PSA", [128, 2560], F32))
        PSB = ctx.enter_context(nc.psum_tensor("PSB", [128, 512], F32))
        PST = ctx.enter_context(nc.psum_tensor("PST", [128, 2048], BF16))

        def bank(i):
            return PSA[:, i * 512:(i + 1) * 512]

        def pmax_chain(pref, src, mslot, rB_t, sB_t, init_waits):
            P.do("tensor", lambda t: t.transpose(PSB[0:1, 128:256], src, idF[:]),
                 waits=init_waits)
            P.do("tensor", lambda t: t.matmul(
                 PSB[0:1, 0:1], ones1[0:1, 0:1], ones1[0:1, 0:1], start=True, stop=True),
                 incs=[(pref + "fx", 1)])
            P.do("vector", lambda v: v.tensor_copy(out=rowb[:], in_=PSB[0:1, 128:256]),
                 waits=[(pref + "fx", 1)], incs=[(pref + "rd", 1)])
            P.do("vector", lambda v, mslot=mslot: v.reduce_max(
                 out=mrow[0:1, mslot:mslot + 1], in_=rowb[:], axis=AX.X),
                 waits=[(pref + "rd", 1)], incs=[(pref + "rd", 1)])
            P.do("tensor", lambda t, mslot=mslot: t.matmul(
                 PSB[:, 256:257], ones1[:], mrow[0:1, mslot:mslot + 1], start=True, stop=True),
                 waits=[(pref + "rd", 2)])
            P.do("tensor", lambda t: t.matmul(
                 PSB[0:1, 384:385], ones1[0:1, 0:1], ones1[0:1, 0:1], start=True, stop=True),
                 incs=[(pref + "f2", 1)])
            P.do("vector", lambda v: v.tensor_copy(out=mtmp[:], in_=PSB[:, 256:257]),
                 waits=[(pref + "f2", 1)], incs=[(pref + "ch", 1)])
            P.do("vector", lambda v: v.tensor_scalar_max(out=mcl[:], in0=mtmp[:], scalar1=MIN_RANGE),
                 waits=[(pref + "ch", 1)], incs=[(pref + "ch", 1)])
            P.do("vector", lambda v: v.reciprocal(out=mrc[:], in_=mcl[:]),
                 waits=[(pref + "ch", 2)], incs=[(pref + "ch", 1)])
            P.do("vector", lambda v, rB_t=rB_t: v.tensor_scalar_mul(out=rB_t[:], in0=mrc[:], scalar1=QP),
                 waits=[(pref + "ch", 3)], incs=[(pref + "ch", 1)])
            P.do("vector", lambda v, sB_t=sB_t: v.tensor_scalar_mul(out=sB_t[:], in0=mcl[:], scalar1=1.0 / QP),
                 waits=[(pref + "ch", 4)], incs=[(pref + "ch", 1)])

        xs = [at("xs0", [128, T], F32, S0), at("xs1", [128, T], F32, S1)]
        sumrow = at("sumrow", [1, T], F32, WQR)
        sqrow = at("sqrow", [1, T], F32, WQR + 16 * KB)
        vrow = at("vrow", [1, T], F32, WQR + 32 * KB)
        sqb = [at("sqb0", [128, TBW], F32, XQR + 32 * KB),
               at("sqb1", [128, TBW], F32, XQR + 34 * KB)]
        wfs = [at("wfs0", [128, D], F32, S2), at("wfs1", [128, D], F32, S2 + 4 * KB)]
        wbs = at("wbs", [128, D], BF16, S2 + 8 * KB)
        xnb = [at("xnb0", [128, KT, TBW], BF16, S2),
               at("xnb1", [128, KT, TBW], BF16, S2 + 8 * KB)]
        c0B = at("c0B", [128, TBW], F32, S3)
        c1B = at("c1B", [128, TBW], F32, S3 + 2 * KB)
        tmpf = [at("tmpf0", [128, TBW], F32, S3 + 8 * KB),
                at("tmpf1", [128, TBW], F32, S3 + 10 * KB)]
        qst = [at("qst0", [128, TBW], BF16, S3 + 12 * KB),
               at("qst1", [128, TBW], BF16, S3 + 13 * KB)]
        kvt = [at("kvt0", [128, TBW], BF16, S3 + 14 * KB),
               at("kvt1", [128, TBW], BF16, S3 + 15 * KB)]
        xqv = [at(f"xq{k}", [128, T], BF16, XQR + k * 8 * KB) for k in range(KT)]
        Wqb = [at(f"Wqb{k}", [128, D], BF16, XQR + k * 2 * KB) for k in range(KT)]
        Wkvb = [at(f"Wkvb{k}", [128, D], BF16, XQR + 16 * KB + k * 2 * KB) for k in range(KT)]
        AQ = WQR + 32 * KB
        qhp = [at("qhp0", [128, SEQ], BF16, AQ), at("qhp1", [128, SEQ], BF16, AQ + 4 * KB)]
        khp = [at("khp0", [128, SEQ], BF16, AQ + 8 * KB), at("khp1", [128, SEQ], BF16, AQ + 12 * KB)]
        tokhp = [at("tok0", [128, TC, 128], BF16, AQ + 16 * KB), at("tok1", [128, TC, 128], BF16, AQ + 20 * KB)]
        outTT = [at(f"oTT{f}", [128, SEQ], BF16, S2 + f * 4 * KB) for f in range(FT)]
        wst = [at("wst0", [128, 8, 128], BF16, AQ + 24 * KB), at("wst1", [128, 8, 128], BF16, AQ + 26 * KB)]
        wls = [at("wls0", [128, J], F32, S0), at("wls1", [128, J], F32, S1)]
        w1q = [at(f"w1q{k}", [128, J], BF16, WQR + k * 8 * KB) for k in range(KT)]
        hst = [at(f"hst{i}", [128, TBW], F32, S2 + i * 2 * KB) for i in range(4)]
        hld = [at(f"hld{i}", [128, 2048], F32, S0 + i * 8 * KB) for i in range(2)]
        hqv = at("hq", [128, 16384], BF16, S1)
        w2q = [at(f"w2q{j}", [128, D], BF16, WQR + j * 2 * KB) for j in range(JT)]
        w2ls = [at("w2ls0", [128, D], F32, S3), at("w2ls1", [128, D], F32, S3 + 4 * KB)]
        atts = at("atts", [128, 8, TBW], BF16, S3)
        xsl = [at("xsl0", [128, TBW], F32, S3 + 8 * KB), at("xsl1", [128, TBW], F32, S3 + 10 * KB)]
        yst = [at("yst0", [128, TBW], F32, S3 + 12 * KB), at("yst1", [128, TBW], F32, S3 + 14 * KB)]

        P = Prog(nc, ctx, paranoid=False)
        paranoid = set(phases if paranoid is True else (paranoid or ()))
        xTr = xT.rearrange("(k p) (n w) -> n p k w", p=128, w=TBW)

        def x_load(buf, tb, waits):
            P.do("sync", lambda s, buf=buf, tb=tb: s.dma_start(
                out=xs[buf].rearrange("p (k w) -> p k w", w=TBW), in_=xTr[tb]),
                waits=waits, incs=[("xl", 16)])

        for t_ in (maxacc, w1acc, w2acc, hacc):
            P.do("vector", lambda v, t_=t_: v.memset(t_[:], 0.0))
        P.do("vector", lambda v: v.memset(epsB[:], LN_EPS), incs=[("cst", 1)])
        P.do("vector", lambda v: v.memset(ones1[:], 1.0))
        P.do("vector", lambda v: v.memset(onesC[:], 1.0))
        P.do("vector", lambda v: v.memset(ccs[:], 0.0), incs=[("cst", 1)])
        P.do("sync", lambda s: s.dma_start(out=idF[:], in_=identF[:]), incs=[("ini", 16)])
        P.do("sync", lambda s: s.dma_start(out=idB[:], in_=identB[:]), incs=[("ini", 16)])
        P.do("sync", lambda s: s.dma_start(out=b1T[:], in_=b1m[:]), incs=[("bte", 16)])
        P.do("sync", lambda s: s.dma_start(out=b2T[:], in_=b2m[:]), incs=[("bte", 16)])
        P.do("sync", lambda s: s.dma_start(out=boutT[:], in_=boutm[:]), incs=[("bte", 16)])
        for k in range(KT):
            for wsrc, wdst, nm in ((Wq, Wqb, "q"), (Wkv, Wkvb, "kv")):
                wi = P.n("wql") // 16
                P.do("sync", lambda s, k=k, wsrc=wsrc, wi=wi: s.dma_start(
                    out=wfs[wi % 2][:], in_=wsrc[k * 128:(k + 1) * 128, :]),
                    waits=[("wqc", max(0, wi - 1))], incs=[("wql", 16)])
                P.do("vector", lambda v, k=k, wdst=wdst, wi=wi: v.tensor_copy(
                    out=wdst[k][:], in_=wfs[wi % 2][:]),
                    waits=[("wql", 16 * (wi + 1))], incs=[("wqc", 1)])
        for k in range(KT):
            wi = P.n("wql") // 16
            P.do("sync", lambda s, k=k, wi=wi: s.dma_start(
                out=wfs[wi % 2][:], in_=Wout[k * 128:(k + 1) * 128, :]),
                waits=[("wqc", max(0, wi - 1)), ("wos", max(0, 16 * (k - 1)))],
                incs=[("wql", 16)])
            P.do("vector", lambda v, k=k, wi=wi: v.tensor_copy(
                out=wbs[:], in_=wfs[wi % 2][:]),
                waits=[("wql", 16 * (wi + 1)), ("wos", max(0, 16 * k))], incs=[("wqc", 1)])
            P.do("sync", lambda s, k=k: s.dma_start(
                out=wouts[:, k].rearrange("dt p d -> p dt d"),
                in_=wbs.rearrange("p (dt d) -> p dt d", d=128)),
                waits=[("wqc", P.n("wqc"))], incs=[("wos", 16)])

        P.paranoid = "stats" in paranoid
        if "stats" in phases:
            xl0 = P.n("xl")
            for tb in range(NTB):
                pb = tb % 2
                x_load(pb, tb, [("smm", max(0, 16 * (tb - 1))), ("sqd", max(0, 8 * (tb - 1)))])
                for k in range(KT):
                    sl = k % 2
                    P.do("scalar", lambda a, pb=pb, k=k, sl=sl: a.square(
                        out=sqb[sl][:], in_=xs[pb][:, k * TBW:(k + 1) * TBW]),
                        waits=[("xl", xl0 + 16 * (tb + 1)), ("smm", max(0, 16 * tb + 2 * k - 2))],
                        incs=[("sqd", 1)])
                    P.do("tensor", lambda t, tb=tb, pb=pb, k=k: t.matmul(
                        PSA[0:1, (tb % 2) * 512:(tb % 2) * 512 + 512],
                        onesC[:], xs[pb][:, k * TBW:(k + 1) * TBW],
                        start=(k == 0), stop=(k == KT - 1)),
                        waits=[("xl", xl0 + 16 * (tb + 1)), ("sev", max(0, 2 * (tb - 1)))],
                        incs=[("smm", 1)])
                    P.do("tensor", lambda t, tb=tb, k=k, sl=sl: t.matmul(
                        PSA[0:1, 1024 + (tb % 2) * 512:1024 + (tb % 2) * 512 + 512],
                        onesC[:], sqb[sl][:],
                        start=(k == 0), stop=(k == KT - 1)),
                        waits=[("sqd", 8 * tb + k + 1), ("sev", max(0, 2 * (tb - 1)))],
                        incs=[("smm", 1)])
                P.do("vector", lambda v, tb=tb: v.tensor_copy(
                    out=sumrow[0:1, tb * TBW:(tb + 1) * TBW],
                    in_=PSA[0:1, (tb % 2) * 512:(tb % 2) * 512 + 512]),
                    waits=[("smm", 16 * (tb + 1))], incs=[("sev", 1)])
                P.do("vector", lambda v, tb=tb: v.tensor_copy(
                    out=sqrow[0:1, tb * TBW:(tb + 1) * TBW],
                    in_=PSA[0:1, 1024 + (tb % 2) * 512:1024 + (tb % 2) * 512 + 512]),
                    waits=[("smm", 16 * (tb + 1))], incs=[("sev", 1)])
            P.do("vector", lambda v: v.tensor_scalar_mul(out=sumrow[:], in0=sumrow[:], scalar1=1.0 / D),
                 waits=[("sev", 16)], incs=[("rch", 1)])
            P.do("vector", lambda v: v.tensor_scalar_mul(out=sqrow[:], in0=sqrow[:], scalar1=1.0 / D),
                 waits=[("rch", 1)], incs=[("rch", 1)])
            P.do("vector", lambda v: v.tensor_mul(out=vrow[:], in0=sumrow[:], in1=sumrow[:]),
                 waits=[("xl", xl0 + 16 * NTB), ("rch", 2)], incs=[("rch", 1)])
            P.do("vector", lambda v: v.tensor_sub(out=sqrow[:], in0=sqrow[:], in1=vrow[:]),
                 waits=[("rch", 3)], incs=[("rch", 1)])
            P.do("scalar", lambda a: a.activation(out=vrow[:], in_=sqrow[:], func=AF.Sqrt, bias=epsB[0:1, 0:1]),
                 waits=[("rch", 4), ("cst", 1)], incs=[("rch", 1)])
            P.do("vector", lambda v: v.reciprocal(out=sqrow[:], in_=vrow[:]),
                 waits=[("rch", 5)], incs=[("rch", 1)])
            P.do("vector", lambda v: v.scalar_tensor_tensor(
                 out=sumrow[:], in0=sumrow[:], scalar=-1.0, in1=sqrow[:], op0=ALU.mult, op1=ALU.mult),
                 waits=[("rch", 6)], incs=[("rch", 1)])


        P.paranoid = "proj" in paranoid
        if "proj" in phases:
            xl0 = P.n("xl")
            for tb in range(NTB):
                pb = tb % 2
                b = tb // 4
                x_load(pb, tb, [("pt1", max(0, 8 * (tb - 1))), ("rch", 7 if "stats" in phases else 0)])
                P.do("tensor", lambda t, tb=tb: t.matmul(
                     bank(3), ones1[:], sumrow[0:1, tb * TBW:(tb + 1) * TBW], start=True, stop=True),
                     waits=[("rch", 7 if "stats" in phases else 0), ("pbe", max(0, 2 * tb - 1))],
                     incs=[("pbc", 1)])
                P.do("tensor", lambda t, tb=tb: t.matmul(
                     bank(4), ones1[:], sqrow[0:1, tb * TBW:(tb + 1) * TBW], start=True, stop=True),
                     waits=[("pbe", max(0, 2 * tb))],
                     incs=[("pbc", 1)])
                P.do("vector", lambda v: v.tensor_copy(out=c0B[:], in_=bank(3)),
                     waits=[("pbc", 2 * tb + 1)], incs=[("pbe", 1)])
                P.do("vector", lambda v: v.tensor_copy(out=c1B[:], in_=bank(4)),
                     waits=[("pbc", 2 * tb + 2)], incs=[("pbe", 1)])
                for k in range(KT):
                    tsl = k % 2
                    m = 8 * tb + k
                    P.do("vector", lambda v, pb=pb, k=k, tsl=tsl: v.tensor_mul(
                        out=tmpf[tsl][:], in0=xs[pb][:, k * TBW:(k + 1) * TBW], in1=c1B[:]),
                        waits=[("xl", xl0 + 16 * (tb + 1)), ("pbe", 2 * tb + 2),
                               ("pmx", max(0, m - 1)), ("wos", 128)],
                        incs=[("pt1", 1)])
                    P.do("vector", lambda v, tsl=tsl: v.tensor_add(
                        out=tmpf[tsl][:], in0=tmpf[tsl][:], in1=c0B[:]),
                        waits=[("pt1", 8 * tb + k + 1)], incs=[("pt2", 1)])
                    P.do("vector", lambda v, pb=pb, k=k, tsl=tsl: v.tensor_copy(
                        out=xnb[pb][:, k, :], in_=tmpf[tsl][:]),
                        waits=[("pmm", max(0, 16 * (tb - 1))), ("pt2", 8 * tb + k + 1)],
                        incs=[("pxn", 1)])
                    P.do("vector", lambda v, tsl=tsl: v.tensor_reduce(
                        out=tmpc[:], in_=tmpf[tsl][:], axis=AX.X, op=ALU.max,
                        apply_absolute_value=True),
                        waits=[("pt2", 8 * tb + k + 1)], incs=[("pmr", 1)])
                    P.do("vector", lambda v: v.tensor_max(out=maxacc[:], in0=maxacc[:], in1=tmpc[:]),
                         waits=[("pmr", 8 * tb + k + 1)], incs=[("pmx", 1)])
                for w in range(2):
                    for f in range(FT):
                        g = tb * 16 + w * 8 + f
                        m = tb * 8 + f
                        bk = g % 3
                        for k in range(KT):
                            wv = Wqb if w == 0 else Wkvb
                            P.do("tensor", lambda t, pb=pb, f=f, k=k, bk=bk, wv=wv: t.matmul(
                                bank(bk), wv[k][:, f * 128:(f + 1) * 128], xnb[pb][:, k, :],
                                start=(k == 0), stop=(k == KT - 1)),
                                waits=[("pxn", 8 * tb + k + 1), ("wqc", 16),
                                       ("pev", max(0, g - 2))],
                                incs=([("pmm", 1)] if k == KT - 1 else []))
                        ssl = m % 2
                        stg = qst if w == 0 else kvt
                        if w == 0:
                            sp_prev = ((m - 2) // 8) * 16 + ((m - 2) % 8) + 1 if m >= 2 else 0
                            extra = []
                        else:
                            sp_prev = ((m - 2) // 8) * 16 + 8 + ((m - 2) % 8) + 1 if m >= 2 else 0
                            extra = [("ptr", max(0, m - 1))]
                        P.do("scalar", lambda a, bk=bk, ssl=ssl, stg=stg, w=w: a.activation(
                            out=stg[ssl][:], in_=bank(bk), func=AF.Copy,
                            scale=(0.125 if w == 0 else 1.0)),
                            waits=[("pmm", g + 1), ("psp", 16 * sp_prev)] + extra,
                            incs=[("pev", 1)])
                        dst = qs if w == 0 else kvs
                        P.do("sync", lambda s, f=f, tb=tb, ssl=ssl, stg=stg, dst=dst: s.dma_start(
                            out=dst[f][:, tb * TBW:(tb + 1) * TBW], in_=stg[ssl][:]),
                            waits=[("pev", g + 1)], incs=[("psp", 16)])
                        if w == 1:
                            ph = m % 2
                            for i in range(4):
                                P.do("tensor", lambda t, ssl=ssl, i=i, ph=ph: t.transpose(
                                    PST[:, ph * 1024 + i * 128: ph * 1024 + (i + 1) * 128],
                                    kvt[ssl][:, i * 128:(i + 1) * 128], idB[:]),
                                    waits=[("pev", g + 1), ("pte", max(0, m - 1))])
                            P.do("tensor", lambda t: t.matmul(
                                PSB[0:1, 0:1], ones1[0:1, 0:1], ones1[0:1, 0:1],
                                start=True, stop=True),
                                incs=[("ptr", 1)])
                            P.do("vector", lambda v, ph=ph, ksl=m % 2: v.tensor_copy(
                                out=ktt[ksl][:], in_=PST[:, ph * 1024: ph * 1024 + 512]),
                                waits=[("ptr", m + 1), ("pts", max(0, 16 * (m - 1)))],
                                incs=[("pte", 1)])
                            P.do("sync", lambda s, b=b, f=f, tb=tb, ksl=m % 2: s.dma_start(
                                out=kts[b, f, (tb % 4) * 4:(tb % 4) * 4 + 4].rearrange("c p d -> p c d"),
                                in_=ktt[ksl].rearrange("p (c d) -> p c d", d=128)),
                                waits=[("pte", m + 1)], incs=[("pts", 16)])
            if "dmaxa" in dbg_out:
                P.do("sync", lambda s: s.dma_start(out=dmaxa[:], in_=maxacc[:]),
                     waits=[("pmx", 64)], incs=[("dmz", 16)])
            P.do("vector", lambda v: v.tensor_copy(out=ccs[:, 0:1], in_=maxacc[:]),
                 waits=[("pmx", 64), ("cst", 2)], incs=[("gmx", 1)])
            P.do("sync", lambda s: s.dma_start(out=ccin1[:], in_=ccs[:]),
                 waits=[("gmx", 1)], incs=[("gst", 16)])
            P.do("gpsimd", lambda g: g.collective_compute(
                "AllReduce", ALU.max, replica_groups=[list(range(8))],
                ins=[ccin1[:]], outs=[ccout1[:]]),
                waits=[("gst", 16)], incs=[("cc1", None)])
            P.do("sync", lambda s: s.dma_start(out=ccr[:], in_=ccout1[:]),
                 waits=[("cc1", 1)], incs=[("gld", 16)])
            if "dbgcc" in dbg_out:
                P.do("sync", lambda s: s.dma_start(out=dbgcc[:, 0:16], in_=ccs[:]),
                     waits=[("gmx", 1)], incs=[("dcc", 16)])
                P.do("sync", lambda s: s.dma_start(out=dbgcc[:, 16:32], in_=ccr[:]),
                     waits=[("gld", 16)], incs=[("dcc", 16)])
            pmax_chain("a1", ccr[:, 0:1], 0, rB, sxB,
                       [("gld", 16), ("pev", 128)])
            P.do("vector", lambda v: v.tensor_copy(out=mrow[0:1, 12:13], in_=rB[0:1, :]),
                 waits=[("a1ch", 5)], incs=[("a1d", 1)])
            P.do("vector", lambda v: v.tensor_copy(out=mrow[0:1, 13:14], in_=sxB[0:1, :]),
                 incs=[("a1d", 1)])
            P.do("vector", lambda v: v.tensor_copy(out=mrow[0:1, 14:15], in_=mtmp[0:1, :]),
                 incs=[("a1d", 1)])
            P.do("vector", lambda v: v.tensor_copy(out=mrow[0:1, 15:16], in_=rB[64:65, :]),
                 incs=[("a1d", 1)])

        P.paranoid = "xq" in paranoid
        if "xq" in phases:
            nproj_mm = P.n("pmm")
            xl0 = P.n("xl")
            P.do("vector", lambda v: v.tensor_scalar_mul(out=sumrow[:], in0=sumrow[:],
                                                         scalar1=rB[0:1, 0:1]),
                 waits=[("pbc", 16 if "proj" in phases else 0), ("a1ch", 5)], incs=[("qcs", 1)])
            P.do("vector", lambda v: v.tensor_scalar_mul(out=sqrow[:], in0=sqrow[:],
                                                         scalar1=rB[0:1, 0:1]),
                 waits=[("qcs", 1)], incs=[("qcs", 1)])
            for tb in range(NTB):
                pb = tb % 2
                x_load(pb, tb, [("xqd", max(0, 8 * (tb - 1))), ("pt1", 64 if "proj" in phases else 0)])
                P.do("tensor", lambda t, tb=tb: t.matmul(
                     bank(3), ones1[:], sumrow[0:1, tb * TBW:(tb + 1) * TBW], start=True, stop=True),
                     waits=[("qcs", 2), ("qbe", max(0, 2 * tb - 1)),
                            ("pbe", 16 if "proj" in phases else 0),
                            ("pt1", 64 if "proj" in phases else 0)],
                     incs=[("qbc", 1)])
                P.do("tensor", lambda t, tb=tb: t.matmul(
                     bank(4), ones1[:], sqrow[0:1, tb * TBW:(tb + 1) * TBW], start=True, stop=True),
                     waits=[("qbe", max(0, 2 * tb))],
                     incs=[("qbc", 1)])
                P.do("vector", lambda v: v.tensor_copy(out=c0B[:], in_=bank(3)),
                     waits=[("qbc", 2 * tb + 1)], incs=[("qbe", 1)])
                P.do("vector", lambda v: v.tensor_copy(out=c1B[:], in_=bank(4)),
                     waits=[("qbc", 2 * tb + 2)], incs=[("qbe", 1)])
                for k in range(KT):
                    tsl = k % 2
                    P.do("vector", lambda v, pb=pb, k=k, tsl=tsl: v.tensor_mul(
                        out=tmpf[tsl][:], in0=xs[pb][:, k * TBW:(k + 1) * TBW], in1=c1B[:]),
                        waits=[("xl", xl0 + 16 * (tb + 1)),
                               ("qbe", 2 * tb + 2), ("xqd", max(0, 8 * tb + k - 1))],
                        incs=[("qt1", 1)])
                    P.do("vector", lambda v, tsl=tsl: v.tensor_add(
                        out=tmpf[tsl][:], in0=tmpf[tsl][:], in1=c0B[:]),
                        waits=[("qt1", 8 * tb + k + 1)], incs=[("qt2", 1)])
                    P.do("vector", lambda v, tb=tb, k=k, tsl=tsl: v.tensor_scalar(
                        out=xqv[k][:, tb * TBW:(tb + 1) * TBW], in0=tmpf[tsl][:],
                        scalar1=C_ROUND, scalar2=-C_ROUND, op0=ALU.add, op1=ALU.add),
                        waits=[("pmm", 128), ("qt2", 8 * tb + k + 1)],
                        incs=[("xqd", 1)])

        P.paranoid = "attn" in paranoid
        if "attn" in phases:
            for b in range(2):
                for hp in range(HP):
                    g = b * HP + hp
                    sl = g % 2
                    lw = [("amst", max(0, 64 * (g - 1))), ("avm", max(0, 64 * (g - 1))),
                          ("psp", 16 * 128 if "proj" in phases else 0),
                          ("asp", 512 * b)]
                    P.do("sync", lambda s, b=b, hp=hp, sl=sl: s.dma_start(
                        out=qhp[sl][:], in_=qs[hp][:, b * SEQ:(b + 1) * SEQ]),
                        waits=lw, incs=[("al", 16)])
                    P.do("sync", lambda s, b=b, hp=hp, sl=sl: s.dma_start(
                        out=khp[sl][:], in_=kvs[hp][:, b * SEQ:(b + 1) * SEQ]),
                        incs=[("al", 16)])
                    P.do("sync", lambda s, b=b, hp=hp, sl=sl: s.dma_start(
                        out=tokhp[sl].rearrange("p c d -> p c d"),
                        in_=kts[b, hp].rearrange("c p d -> p c d"),),
                        waits=[("pts", 16 * 64 if "proj" in phases else 0)],
                        incs=[("al", 16)])
                    for qqp in range(2):
                        G2 = g * 2 + qqp
                        def st_pair(tc, s2, qqp=qqp, G2=G2, sl=sl, g=g, hp=hp, b=b):
                            qq = 2 * qqp + s2
                            pidx = (G2 * 16 + tc) * 2 + s2
                            w0 = [("al", 48 * (g + 1)),
                                  ("aev%d" % s2, max(0, G2 * 16 + tc))]
                            if g == 0 and qqp == 0 and tc == 0 and s2 == 0:
                                if "xq" in phases:
                                    w0.append(("qbe", 16))
                                if "proj" in phases:
                                    w0.append(("a1ch", 1))
                            if hp == 0 and qqp == 0 and tc == 0 and s2 == 0:
                                w0.append(("ape", 32 * b))
                            P.do("tensor", lambda t, sl=sl, qq=qq, tc=tc, s2=s2: t.matmul(
                                bank(2 * s2), khp[sl][0:64, tc * 128:(tc + 1) * 128],
                                qhp[sl][0:64, qq * TBW:(qq + 1) * TBW],
                                start=True, stop=True, tile_position=(0, 0)),
                                waits=w0)
                            P.do("tensor", lambda t, sl=sl, qq=qq, tc=tc, s2=s2: t.matmul(
                                bank(2 * s2 + 1), khp[sl][64:128, tc * 128:(tc + 1) * 128],
                                qhp[sl][64:128, qq * TBW:(qq + 1) * TBW],
                                start=True, stop=True, tile_position=(64, 0)),
                                incs=[("amst", 1)])

                        def evac(tc, s2, G2=G2):
                            eng = "scalar" if s2 == 0 else "vector"
                            bsl = 2 * s2 + tc % 2
                            w0 = [("amst", (G2 * 16 + tc) * 2 + s2 + 1),
                                  ("avm", max(0, (G2 * 16 + tc - 2) * 2 + s2 + 1))]
                            if eng == "scalar":
                                P.do("scalar", lambda a, s2=s2, bsl=bsl: a.activation(
                                    out=stbuf[bsl][:], in_=PSA[:, s2 * 1024: s2 * 1024 + 1024],
                                    func=AF.Relu), waits=w0, incs=[("aev0", 1)])
                            else:
                                P.do("vector", lambda v, s2=s2, bsl=bsl: v.tensor_scalar_max(
                                    out=stbuf[bsl][:], in0=PSA[:, s2 * 1024: s2 * 1024 + 1024],
                                    scalar1=0.0), waits=w0, incs=[("aev1", 1)])

                        def av_pair(tc, s2, G2=G2, sl=sl):
                            avt = PSA[:, 2048:2560] if s2 == 0 else PSB[:, 0:512]
                            bsl = 2 * s2 + tc % 2
                            w0 = [("aev%d" % s2, G2 * 16 + tc + 1),
                                  ("aoe%d" % s2, max(0, G2))]
                            P.do("tensor", lambda t, sl=sl, tc=tc, bsl=bsl, avt=avt: t.matmul(
                                avt[0:64, :], tokhp[sl][:, tc, 0:64], stbuf[bsl][:, 0:TBW],
                                start=(tc == 0), stop=(tc == TC - 1), tile_position=(0, 0)),
                                waits=w0)
                            P.do("tensor", lambda t, sl=sl, tc=tc, bsl=bsl, avt=avt: t.matmul(
                                avt[64:128, :], tokhp[sl][:, tc, 64:128], stbuf[bsl][:, TBW:2 * TBW],
                                start=(tc == 0), stop=(tc == TC - 1), tile_position=(0, 64)),
                                incs=[("avm", 1)])

                        for tc in range(TC):
                            for s2 in range(2):
                                st_pair(tc, s2)
                            for s2 in range(2):
                                evac(tc, s2)
                            if tc >= 1:
                                for s2 in range(2):
                                    av_pair(tc - 1, s2)
                        for s2 in range(2):
                            av_pair(TC - 1, s2)
                        for s2 in range(2):
                            qq = 2 * qqp + s2
                            avt = PSA[:, 2048:2560] if s2 == 0 else PSB[:, 0:512]
                            eng = "scalar" if s2 == 0 else "vector"
                            P.do(eng, (lambda e, hp=hp, qq=qq, avt=avt, eng=eng:
                                       (e.copy if eng == "scalar" else e.tensor_copy)(
                                           out=outTT[hp][:, qq * TBW:(qq + 1) * TBW], in_=avt)),
                                 waits=[("avm", (G2 * 16 + TC - 1) * 2 + s2 + 1), ("apm", 32 * b),
                                        ("xqd", 64 if "xq" in phases else 0)],
                                 incs=[("aoe%d" % s2, 1)])
                for dt in range(8):
                    gd = b * 8 + dt
                    wsl = gd % 2
                    P.do("sync", lambda s, dt=dt, wsl=wsl: s.dma_start(
                        out=wst[wsl].rearrange("p f d -> p f d"),
                        in_=wouts[dt].rearrange("f p d -> p f d")),
                        waits=[("wos", 128), ("apm", max(0, 4 * (gd - 1)))],
                        incs=[("awl", 16)])
                    for qq in range(QQ):
                        gq = gd * QQ + qq
                        obk = gq % 2
                        for f in range(FT):
                            P.do("tensor", lambda t, f=f, qq=qq, wsl=wsl, obk=obk: t.matmul(
                                bank(obk), wst[wsl][:, f, :], outTT[f][:, qq * TBW:(qq + 1) * TBW],
                                start=(f == 0), stop=(f == FT - 1)),
                                waits=[("awl", 16 * (gd + 1)), ("aoe0", 16 * (b + 1)),
                                       ("aoe1", 16 * (b + 1)), ("ape", max(0, gq - 1))],
                                incs=([("apm", 1)] if f == FT - 1 else []))
                        asl = gq % 2
                        P.do("scalar", lambda a, dt=dt, asl=asl, obk=obk: a.activation(
                            out=aos[asl][:], in_=bank(obk), func=AF.Identity, bias=boutT[:, dt:dt + 1]),
                            waits=[("apm", gq + 1), ("asp", max(0, 16 * (gq - 1)))],
                            incs=[("ape", 1)])
                        P.do("sync", lambda s, b=b, dt=dt, qq=qq, asl=asl: s.dma_start(
                            out=att_s[b, dt][:, qq * TBW:(qq + 1) * TBW], in_=aos[asl][:]),
                            waits=[("ape", gq + 1)], incs=[("asp", 16)])

        P.paranoid = "ffn1" in paranoid
        if "ffn1" in phases:
            post_attn = [("asp", 1024 if "attn" in phases else 0)]
            for k in range(KT):
                sl = k % 2
                P.do("sync", lambda s, k=k, sl=sl: s.dma_start(
                    out=wls[sl][:], in_=w1T[k * 128:(k + 1) * 128, :]),
                    waits=[("f1wr", max(0, k - 1))] + post_attn,
                    incs=[("f1wl", 16)])
                P.do("vector", lambda v, sl=sl: v.tensor_reduce(
                    out=tmpc[:], in_=wls[sl][:], axis=AX.X, op=ALU.max, apply_absolute_value=True),
                    waits=[("f1wl", 16 * (k + 1)), ("f1wr", max(0, k))], incs=[("f1wr0", 1)])
                P.do("vector", lambda v: v.tensor_max(out=w1acc[:], in0=w1acc[:], in1=tmpc[:]),
                     waits=[("f1wr0", k + 1)], incs=[("f1wr", 1)])
            pmax_chain("w1", w1acc[:], 4, rw1B, sw1B,
                       [("f1wr", KT), ("apm", 64 if "attn" in phases else 0),
                        ("aoe0", 32 if "attn" in phases else 0),
                        ("aoe1", 32 if "attn" in phases else 0)])
            P.do("vector", lambda v: v.tensor_mul(out=s1B[:], in0=sw1B[:], in1=sxB[:]),
                 waits=[("w1ch", 5), ("a1ch", 5 if "proj" in phases else 0)], incs=[("f1sc", 1)])
            P.do("vector", lambda v: v.tensor_copy(out=mrow[0:1, 5:6], in_=s1B[0:1, :]), incs=[("dsc", 1)])
            P.do("vector", lambda v: v.tensor_copy(out=mrow[0:1, 6:7], in_=rB[0:1, :]), incs=[("dsc", 1)])
            P.do("vector", lambda v: v.tensor_copy(out=mrow[0:1, 7:8], in_=sxB[0:1, :]), incs=[("dsc", 1)])
            P.do("vector", lambda v: v.tensor_copy(out=mrow[0:1, 3:4], in_=rw1B[0:1, :]), incs=[("dsc", 1)])
            P.do("vector", lambda v: v.tensor_scalar_mul(out=b1s[:], in0=b1T[:], scalar1=s1B[:]),
                 waits=[("f1sc", 1), ("bte", 48)], incs=[("f1be", 3)])
            for k in range(KT):
                sl = k % 2
                P.do("sync", lambda s, k=k, sl=sl: s.dma_start(
                    out=wls[sl][:], in_=w1T[k * 128:(k + 1) * 128, :]),
                    waits=[("f1qz", max(0, k - 1)), ("f1be", 3),
                           ("qbc", 16 if "xq" in phases else 0)],
                    incs=[("f1w2", 16)])
                P.do("scalar", lambda a, sl=sl: a.activation(
                    out=wls[sl][:], in_=wls[sl][:], func=AF.Copy, scale=rw1B[:], bias=C_ROUND),
                    waits=[("f1w2", 16 * (k + 1))], incs=[("f1qa", 1)])
                P.do("vector", lambda v, k=k, sl=sl: v.tensor_scalar_add(
                    out=w1q[k][:], in0=wls[sl][:], scalar1=-C_ROUND),
                    waits=[("f1qa", k + 1)], incs=[("f1qz", 1)])
            for tb in range(NTB):
                for jt in range(JT):
                    seq = tb * JT + jt
                    bk = seq % 5
                    for k in range(KT):
                        P.do("tensor", lambda t, tb=tb, jt=jt, k=k, bk=bk: t.matmul(
                            bank(bk), w1q[k][:, jt * 128:(jt + 1) * 128],
                            xqv[k][:, tb * TBW:(tb + 1) * TBW],
                            start=(k == 0), stop=(k == KT - 1)),
                            waits=[("f1qz", KT), ("xqd", 64),
                                   ("f1e", max(0, seq - 4))],
                            incs=([("f1m", 1)] if k == KT - 1 else []))
                    ssl = seq % 4
                    P.do("scalar", lambda a, bk=bk, ssl=ssl, jt=jt: a.activation(
                        out=hst[ssl][:], in_=bank(bk), func=AF.Gelu,
                        scale=s1B[:], bias=b1s[:, jt:jt + 1]),
                        waits=[("f1m", seq + 1), ("f1s", max(0, 16 * (seq - 3))),
                               ("f1r", max(0, 2 * (seq - 3)))],
                        incs=[("f1e", 1)])
                    P.do("sync", lambda s, jt=jt, tb=tb, ssl=ssl: s.dma_start(
                        out=hsp[jt][:, tb * TBW:(tb + 1) * TBW], in_=hst[ssl][:]),
                        waits=[("f1e", seq + 1)], incs=[("f1s", 16)])
                    P.do("vector", lambda v, ssl=ssl: v.tensor_reduce(
                        out=tmpc[:], in_=hst[ssl][:], axis=AX.X, op=ALU.max,
                        apply_absolute_value=True),
                        waits=[("f1e", seq + 1), ("f1r", max(0, 2 * seq))], incs=[("f1r0", 1)])
                    P.do("vector", lambda v: v.tensor_max(out=hacc[:], in0=hacc[:], in1=tmpc[:]),
                         waits=[("f1r0", seq + 1)], incs=[("f1r", 2)])
            P.do("vector", lambda v: v.tensor_copy(out=ccs[:, 0:1], in_=hacc[:]),
                 waits=[("f1r", 2 * NTB * JT)], incs=[("ghx", 1)])
            P.do("sync", lambda s: s.dma_start(out=ccin2[:], in_=ccs[:]),
                 waits=[("ghx", 1)], incs=[("gst2", 16)])
            P.do("gpsimd", lambda g: g.collective_compute(
                "AllReduce", ALU.max, replica_groups=[list(range(8))],
                ins=[ccin2[:]], outs=[ccout2[:]]),
                waits=[("gst2", 16)], incs=[("cc2", None)])
            P.do("sync", lambda s: s.dma_start(out=ccr[:], in_=ccout2[:]),
                 waits=[("cc2", 1)], incs=[("gld2", 16)])
            pmax_chain("hh", ccr[:, 0:1], 8, rhB, shB,
                       [("gld2", 16), ("f1e", 256)])
            if "dxq" in dbg_out:
                for k in range(KT):
                    P.do("sync", lambda s, k=k: s.dma_start(out=dxq[k], in_=xqv[k][:]),
                         waits=[("xqd", 64), ("f1qz", KT)], incs=[("ddmp", 16)])
                    P.do("sync", lambda s, k=k: s.dma_start(out=dw1q[k], in_=w1q[k][:]),
                         incs=[("ddmp", 16)])
                P.do("gpsimd", lambda g: g.dma_start(out=mxs[:], in_=mrow[:]),
                     waits=[("gld2", 16), ("dsc", 4)], incs=[("ddmp", 16)])
                P.do("sync", lambda s: None, waits=[("ddmp", P.n("ddmp"))])

        P.paranoid = "ffn2" in paranoid
        if "ffn2" in phases:
            for j in range(JT):
                sl = j % 2
                P.do("sync", lambda s, j=j, sl=sl: s.dma_start(
                    out=w2ls[sl][:], in_=w2T[j * 128:(j + 1) * 128, :]),
                    waits=[("f2wr", max(0, j - 1)),
                           ("apm", 64 if "attn" in phases else 0)],
                    incs=[("f2wl", 16)])
                P.do("vector", lambda v, sl=sl: v.tensor_reduce(
                    out=tmpc[:], in_=w2ls[sl][:], axis=AX.X, op=ALU.max, apply_absolute_value=True),
                    waits=[("f2wl", 16 * (j + 1)), ("f2wr", max(0, j))], incs=[("f2wr0", 1)])
                P.do("vector", lambda v: v.tensor_max(out=w2acc[:], in0=w2acc[:], in1=tmpc[:]),
                     waits=[("f2wr0", j + 1)], incs=[("f2wr", 1)])
            pmax_chain("w2", w2acc[:], 12, rw2B, sw2B,
                       [("f2wr", JT), ("f1e", 256 if "ffn1" in phases else 0),
                        ("hhch", 5 if "ffn1" in phases else 0)])
            P.do("vector", lambda v: v.tensor_mul(out=s2B[:], in0=sw2B[:], in1=shB[:]),
                 waits=[("w2ch", 5), ("hhch", 5 if "ffn1" in phases else 0)], incs=[("f2sc", 1)])
            P.do("vector", lambda v: v.tensor_scalar_mul(out=b2s[:], in0=b2T[:], scalar1=s2B[:]),
                 waits=[("f2sc", 1), ("bte", 48)], incs=[("f2be", 4)])
            for j in range(JT):
                sl = j % 2
                P.do("sync", lambda s, j=j, sl=sl: s.dma_start(
                    out=w2ls[sl][:], in_=w2T[j * 128:(j + 1) * 128, :]),
                    waits=[("f2qz", max(0, j - 1)), ("f2be", 4)],
                    incs=[("f2w2", 16)])
                P.do("scalar", lambda a, sl=sl: a.activation(
                    out=w2ls[sl][:], in_=w2ls[sl][:], func=AF.Copy, scale=rw2B[:], bias=C_ROUND),
                    waits=[("f2w2", 16 * (j + 1))], incs=[("f2qa", 1)])
                P.do("vector", lambda v, j=j, sl=sl: v.tensor_scalar_add(
                    out=w2q[j][:], in0=w2ls[sl][:], scalar1=-C_ROUND),
                    waits=[("f2qa", j + 1)], incs=[("f2qz", 1)])
            hspr = hsp.rearrange("(g c) p w -> g c p w", c=4)
            for tb in range(NTB):
                b = tb // 4
                qcol = (tb % 4) * TBW
                for jg in range(8):
                    u = tb * 8 + jg
                    sl = u % 2
                    P.do("sync", lambda s, tb=tb, jg=jg, sl=sl: s.dma_start(
                        out=hld[sl].rearrange("p (c w) -> p c w", w=TBW),
                        in_=hspr[jg][:, :, tb * TBW:(tb + 1) * TBW].rearrange("c p w -> p c w")),
                        waits=[("hqd", max(0, u - 1)),
                               ("f1s", 16 * NTB * JT if "ffn1" in phases else 0)],
                        incs=[("hxl", 16)])
                    P.do("scalar", lambda a, sl=sl: a.activation(
                        out=hld[sl][:], in_=hld[sl][:], func=AF.Copy, scale=rhB[:], bias=C_ROUND),
                        waits=[("hxl", 16 * (u + 1)), ("f2be", 4)], incs=[("hqa", 1)])
                    P.do("vector", lambda v, jg=jg, sl=sl: v.tensor_scalar_add(
                        out=hqv[:, jg * 2048:(jg + 1) * 2048], in0=hld[sl][:], scalar1=-C_ROUND),
                        waits=[("hqa", u + 1), ("f2m", max(0, 8 * (tb - 1)))],
                        incs=[("hqd", 1)])
                P.do("sync", lambda s, b=b, qcol=qcol: s.dma_start(
                    out=atts[:],
                    in_=att_s[b][:, :, qcol:qcol + TBW].rearrange("dt p w -> p dt w")),
                    waits=[("asp", 1024 if "attn" in phases else 0),
                           ("f2qz", JT),
                           ("yev", max(0, 16 * (tb - 1)))],
                    incs=[("f2al", 16)])
                for dt in range(8):
                    seq = tb * 8 + dt
                    bk = seq % 4
                    xsl_sl = seq % 2
                    P.do("sync", lambda s, dt=dt, tb=tb, xsl_sl=xsl_sl: s.dma_start(
                        out=xsl[xsl_sl][:], in_=xT[dt * 128:(dt + 1) * 128, tb * TBW:(tb + 1) * TBW]),
                        waits=[("yev", max(0, 2 * seq - 2))],
                        incs=[("f2xl", 16)])
                    for jt in range(JT):
                        P.do("tensor", lambda t, jt=jt, dt=dt, bk=bk: t.matmul(
                            bank(bk), w2q[jt][:, dt * 128:(dt + 1) * 128],
                            hqv[:, jt * TBW:(jt + 1) * TBW],
                            start=(jt == 0), stop=(jt == JT - 1)),
                            waits=[("hqd", 8 * (tb + 1)), ("f2qz", JT),
                                   ("f2e", max(0, seq - 3))],
                            incs=([("f2m", 1)] if jt == JT - 1 else []))
                    ysl = seq % 2
                    P.do("scalar", lambda a, bk=bk, dt=dt, ysl=ysl: a.activation(
                        out=yst[ysl][:], in_=bank(bk), func=AF.Identity, scale=s2B[:],
                        bias=b2s[:, dt:dt + 1]),
                        waits=[("f2m", seq + 1), ("yw", max(0, 16 * (seq - 1)))],
                        incs=[("f2e", 1)])
                    P.do("vector", lambda v, dt=dt, ysl=ysl: v.tensor_add(
                        out=yst[ysl][:], in0=yst[ysl][:], in1=atts[:, dt, :]),
                        waits=[("f2e", seq + 1), ("f2al", 16 * (tb + 1))],
                        incs=[("yev", 1)])
                    P.do("vector", lambda v, ysl=ysl, xsl_sl=xsl_sl: v.tensor_add(
                        out=yst[ysl][:], in0=yst[ysl][:], in1=xsl[xsl_sl][:]),
                        waits=[("f2xl", 16 * (seq + 1)), ("yev", 2 * seq + 1)],
                        incs=[("yev", 1)])
                    P.do("sync", lambda s, dt=dt, tb=tb, ysl=ysl: s.dma_start(
                        out=yT[dt * 128:(dt + 1) * 128, tb * TBW:(tb + 1) * TBW], in_=yst[ysl][:]),
                        waits=[("yev", 2 * (seq + 1))], incs=[("yw", 16)])
            P.do("sync", lambda s: None, waits=[("yw", 16 * 64)])
        else:
            for snm in ("psp", "pts", "asp", "f1s", "wos"):
                if P.n(snm):
                    P.do("sync", lambda e, snm=snm: None, waits=[(snm, P.n(snm))])
            if "proj" in phases:
                P.do("gpsimd", lambda g: g.dma_start(out=mxs[:], in_=mrow[:]),
                     waits=[("a1ch", 5), ("a1d", 4)], incs=[("dmx", 16)])
                P.do("sync", lambda s: None, waits=[("dmx", 16)])
            if "xq" in phases and "dxq" in dbg_out:
                for k in range(KT):
                    P.do("sync", lambda s, k=k: s.dma_start(out=dxq[k], in_=xqv[k][:]),
                         waits=[("xqd", 64)], incs=[("ddmp", 16)])
                P.do("sync", lambda s: None, waits=[("ddmp", P.n("ddmp"))])

        with nc.Block() as block:
            P.replay(block)

    return nc


def make_inputs(x, Wq, Wkv, Wout, bout, w1, b1, w2, b2):
    import ml_dtypes
    w1T = np.ascontiguousarray(w1.T)
    w2T = np.ascontiguousarray(w2.T)
    b1m = np.ascontiguousarray(b1.reshape(32, 128).T)
    b2m = np.ascontiguousarray(b2.reshape(8, 128).T)
    boutm = np.ascontiguousarray(bout.reshape(8, 128).T)
    identF = np.eye(128, dtype=np.float32)
    identB = np.eye(128, dtype=ml_dtypes.bfloat16)
    in_maps = []
    for c in range(8):
        xTc = np.ascontiguousarray(x[2 * c:2 * c + 2].transpose(2, 0, 1).reshape(D, T))
        in_maps.append({
            "xT": xTc, "Wq": Wq, "Wkv": Wkv, "Wout": Wout,
            "w1T": w1T, "w2T": w2T, "b1m": b1m, "b2m": b2m, "boutm": boutm,
            "identF": identF, "identB": identB,
        })
    return in_maps


def kernel(L0_attn_hidden_states, L0_attn_Wq, L0_attn_Wkv, L0_attn_Wout, L0_attn_bout,
           L0_ffn_w1, L0_ffn_b1, L0_ffn_w2, L0_ffn_b2):
    x = np.asarray(L0_attn_hidden_states, dtype=np.float32)
    Wq = np.ascontiguousarray(np.asarray(L0_attn_Wq, dtype=np.float32))
    Wkv = np.ascontiguousarray(np.asarray(L0_attn_Wkv, dtype=np.float32))
    Wout = np.ascontiguousarray(np.asarray(L0_attn_Wout, dtype=np.float32))
    bout = np.asarray(L0_attn_bout, dtype=np.float32)
    w1 = np.asarray(L0_ffn_w1, dtype=np.float32)
    b1 = np.asarray(L0_ffn_b1, dtype=np.float32)
    w2 = np.asarray(L0_ffn_w2, dtype=np.float32)
    b2 = np.asarray(L0_ffn_b2, dtype=np.float32)

    nc = build_nc()
    in_maps = make_inputs(x, Wq, Wkv, Wout, bout, w1, b1, w2, b2)
    res = run_bass_kernel_spmd(nc, in_maps, core_ids=list(range(8)))
    y = np.empty((16, 2048, 1024), dtype=np.float32)
    for c in range(8):
        yTc = np.asarray(res.results[c]["yT"])
        y[2 * c:2 * c + 2] = yTc.reshape(D, 2, SEQ).transpose(1, 2, 0)
    return y
